# version 06
# baseline (speedup 1.0000x reference)
import numpy as np
import ml_dtypes
import concourse.bass as bass
import concourse.mybir as mybir
from concourse.bass_utils import run_bass_kernel_spmd
from contextlib import ExitStack

F32 = mybir.dt.float32
BF16 = mybir.dt.bfloat16
I32 = mybir.dt.int32
AF = mybir.ActivationFunctionType
AX = mybir.AxisListType

D = 4096
SC = 1024
NKV = 8
HD = 128
HF = 11008
KF2 = 2752
NCORE = 8
INVSQ = 0.08838834764831843
LORA_SCALE = 0.5

STAGE = 5


class Plan:
    def __init__(self):
        self.ops = []
        self.sem = {}

    def cnt(self, name):
        return self.sem.get(name, 0)

    def emit(self, eng, fn=None, waits=(), inc=None):
        ws = [(s, v) for (s, v) in waits if v > 0]
        incs = []
        if inc is not None:
            for (s, amt) in ([inc] if isinstance(inc, tuple) else inc):
                self.sem[s] = self.sem.get(s, 0) + amt
                incs.append((s, amt))
        self.ops.append((eng, fn, ws, incs))


def build_plan(P, tn):
    E = P.emit
    C = P.cnt
    QT = tn["QT"]; BIG = tn["BIG"]; SLAB = tn["SLAB"]; MISC = tn["MISC"]
    FSCR = tn["FSCR"]
    S0 = tn["S0"]; S1 = tn["S1"]; ACC = tn["ACC"]; BCAST = tn["BCAST"]
    ROW = tn["ROW"]; PTB = tn["PTB"]
    cos = tn["cos"]; sin = tn["sin"]; onesc = tn["onesc"]; onesr = tn["onesr"]
    ident = tn["ident"]; perm = tn["perm"]
    idx_sb = tn["idx_sb"]; bias_sb = tn["bias_sb"]
    maxq_list = tn["maxq_list"]; maxk_list = tn["maxk_list"]
    maxq = tn["maxq"]; srow = tn["srow"]; statf = tn["statf"]; statb = tn["statb"]
    lrec = tn["lrec"]; lrecb = tn["lrecb"]; dena = tn["dena"]; denb = tn["denb"]
    a1t = tn["a1t"]; a3t = tn["a3t"]; a2t = tn["a2t"]
    b1t = tn["b1t"]; b3t = tn["b3t"]; b2t = tn["b2t"]
    u1b = tn["u1b"]; u3b = tn["u3b"]; u2b = tn["u2b"]
    hsT = tn["hsT"]; wq = tn["wq"]; wo = tn["wo"]; kvc = tn["kvc"]; bt = tn["bt"]
    w1qT = tn["w1qT"]; w3qT = tn["w3qT"]; w2qT = tn["w2qT"]
    outT = tn["outT"]

    E("gpsimd", lambda e: e.dma_start(out=idx_sb[:, :], in_=bt.rearrange("(g p) -> p g", p=128),
                                      allow_slow_non_contiguous=True), inc=("G", 16))
    E("gpsimd", lambda e: e.dma_start(out=a1t[:, :], in_=tn["a1th"][:, :]), inc=("G", 16))
    E("gpsimd", lambda e: e.dma_start(out=a3t[:, :], in_=tn["a3th"][:, :]), inc=("G", 16))
    E("gpsimd", lambda e: e.dma_start(out=a2t[:, :], in_=tn["a2th"][:, :]), inc=("G", 16))
    g_consts = C("G")
    E("sync", lambda e: e.dma_start(out=cos[:, :], in_=tn["rc_h"][:, :]), inc=("S", 16))
    E("sync", lambda e: e.dma_start(out=sin[:, :], in_=tn["rs_h"][:, :]), inc=("S", 16))
    E("sync", lambda e: e.dma_start(out=onesc[:, :], in_=tn["onesc_h"][:, :]), inc=("S", 16))
    E("sync", lambda e: e.dma_start(out=onesr[:, :], in_=tn["onesr_h"][:, :]), inc=("S", 16))
    E("sync", lambda e: e.dma_start(out=ident[:, :], in_=tn["ident_h"][:, :]), inc=("S", 16))
    E("sync", lambda e: e.dma_start(out=perm[:, :], in_=tn["perm_h"][:, :]), inc=("S", 16))
    s_consts = C("S")

    for c in range(32):
        E("gpsimd", lambda e, c=c: e.dma_start(out=BIG[:, c * 1024:(c + 1) * 1024],
                                               in_=hsT[c * 128:(c + 1) * 128, :]), inc=("G", 16))
    g_hs = C("G")

    wq3 = wq.rearrange("(c p) q -> p c q", p=128)
    t_slabfree = {}
    v_qdone = {}
    v_qk = {}
    sc_sq = {}
    qn_idx = 0
    for dq in range(32):
        buf = dq % 2
        waits = [("T", t_slabfree[dq - 2])] if dq >= 2 else []
        E("gpsimd", lambda e, dq=dq, buf=buf: e.dma_start(
            out=SLAB[:, buf * 4096:(buf + 1) * 4096].rearrange("p (c q) -> p c q", q=128),
            in_=wq3[:, :, dq * 128:(dq + 1) * 128]), waits=waits, inc=("G", 16))
        g_slab = C("G")
        ps = [S0, S1][buf]
        for st in range(2):
            for dk in range(32):
                waits = []
                if dk == 0:
                    waits.append(("G", g_slab))
                    if dq == 0 and st == 0:
                        waits.append(("G", g_hs))
                    if dq >= 2:
                        waits.append(("V", v_qdone[(dq - 2, st)]))
                        waits.append(("SC", sc_sq[(dq - 2, st)]))
                E("tensor", lambda e, ps=ps, st=st, buf=buf, dk=dk: e.matmul(
                    ps[:, st * 512:(st + 1) * 512],
                    SLAB[:, buf * 4096 + dk * 128: buf * 4096 + (dk + 1) * 128],
                    BIG[:, dk * 1024 + st * 512: dk * 1024 + st * 512 + 512],
                    start=(dk == 0), stop=(dk == 31), skip_group_check=True),
                    waits=waits,
                    inc=("T", 1) if dk == 31 else None)
            t_q = C("T")
            if st == 1:
                t_slabfree[dq] = t_q
            qsl = ps[:, st * 512:(st + 1) * 512]
            cosl = cos[:, st * 512:(st + 1) * 512]
            sinl = sin[:, st * 512:(st + 1) * 512]
            qa = FSCR[:, 0:512]
            qb = FSCR[:, 512:1024]
            base = dq * 1024 + st * 512
            k = dq * 2 + st
            qsb = MISC[:, 3072 + (k % 2) * 512: 3072 + (k % 2) * 512 + 512]
            ph = ACC[:, :]
            waits = [("T", t_q)]
            if k >= 2:
                waits.append(("V", v_qk[k - 2]))
            E("scalar", lambda e, qsb=qsb, qsl=qsl: e.copy(qsb, qsl), waits=waits,
              inc=("SC", 1))
            sc_ev = C("SC")
            E("scalar", lambda e, qsl=qsl: e.activation(MISC[:, 2560:3072], qsl, AF.Square),
              inc=("SC", 1))
            sc_sq[(dq, st)] = C("SC")
            waits = [("SC", sc_ev), ("S", s_consts)]
            if k >= 1:
                waits.append(("V", v_qk[k - 1]))
            E("tensor", lambda e, ph=ph, qsb=qsb: e.matmul(
                ph, perm[:, 0:128], qsb, start=True, stop=True, skip_group_check=True),
                waits=waits, inc=("T2", 1))
            t_rot = C("T2")
            E("vector", lambda e, qsb=qsb, cosl=cosl, qa=qa: e.tensor_mul(qa, qsb, cosl),
              waits=[("SC", sc_ev)])
            E("vector", lambda e, ph=ph, sinl=sinl, qb=qb: e.tensor_mul(qb, ph, sinl),
              waits=[("T2", t_rot)])
            E("vector", lambda e, qa=qa, qb=qb, base=base: e.tensor_add(
                QT[:, base:base + 512], qa, qb), inc=("V", 1))
            v_qdone[(dq, st)] = C("V")
            v_qk[k] = C("V")
            waits = [("V", v_qdone[(dq, st)]), ("SC", sc_sq[(dq, st)])]
            if qn_idx >= 1:
                waits.append(("V2", qn_idx))
            E("tensor", lambda e: e.matmul(BCAST[0:1, 0:512], onesc[:, :], MISC[:, 2560:3072],
                                           start=True, stop=True, skip_group_check=True),
              waits=waits, inc=("T2", 1))
            t_n = C("T2")
            E("vector", lambda e, i=qn_idx: e.reduce_max(maxq_list[0:1, i:i + 1],
                                                         BCAST[0:1, 0:512], axis=AX.X),
              waits=[("T2", t_n)], inc=("V2", 1))
            qn_idx += 1
    E("vector", lambda e: e.reduce_max(maxq[0:1, 0:1], maxq_list[0:1, 0:64], axis=AX.X),
      waits=[("V2", qn_idx)], inc=("V", 1))
    v_maxq = C("V")
    v_qt_done = C("V")
    t_phaseA = C("T")

    if STAGE == 1:
        dump_and_finish(P, tn, QT, [("V", v_qt_done)])
        return

    g_gather = {}
    t_attn_end = {}
    v_ktdone = {}
    s_bias = {}

    def emit_gather(h, extra_waits=()):
        par = h % 2
        waits = [("G", g_consts)] + list(extra_waits)
        if h >= 2:
            waits.append(("T", t_attn_end[h - 2]))
        kvflat = kvc.rearrange("b c hh s d -> b (c hh s d)")
        for cc in range(2):
            eoff = cc * 16384 + h * 2048
            for g in range(4):
                dst = BIG[:, par * 16384 + cc * 8192 + g * 2048:
                          par * 16384 + cc * 8192 + (g + 1) * 2048]
                E("gpsimd", lambda e, dst=dst, g=g, eoff=eoff: e.indirect_dma_start(
                    dst, None, kvflat,
                    bass.IndirectOffsetOnAxis(ap=idx_sb[:, g:g + 1], axis=0),
                    element_offset=eoff),
                    waits=waits if (cc == 0 and g == 0) else (), inc=("G", 16))
        g_gather[h] = C("G")

    def emit_head_prep(h):
        par = h % 2
        kbase = par * 16384
        for q8 in range(8):
            for c8 in range(8):
                cc = q8 * 8 + c8
                waits = []
                if q8 == 0 and c8 == 0:
                    waits.append(("G", g_gather[h]))
                if c8 == 0 and q8 >= 1:
                    waits.append(("V3", C("V3")))
                E("tensor", lambda e, kbase=kbase, cc=cc, c8=c8: e.transpose(
                    PTB[:, c8 * 128:(c8 + 1) * 128],
                    BIG[:, kbase + cc * 128:kbase + (cc + 1) * 128], ident[:, :]),
                    waits=waits, inc=("T", 1) if c8 == 7 else None)
            t_tr = C("T")
            E("vector", lambda e, kbase=kbase, q8=q8: e.tensor_copy(
                BIG[:, kbase + q8 * 1024:kbase + (q8 + 1) * 1024], PTB[:, :]),
                waits=[("T", t_tr)], inc=("V3", 1))
        v_ktdone[h] = C("V3")
        t_prev_norm = None
        for ch in range(16):
            slot = MISC[:, 2560 + (ch % 2) * 512: 2560 + (ch % 2) * 512 + 512]
            waits = [("V3", v_ktdone[h])]
            if t_prev_norm is not None and ch >= 2:
                waits.append(("T2", t_prev_norm[ch - 2]))
            E("vector", lambda e, kbase=kbase, ch=ch, slot=slot: e.tensor_mul(
                slot, BIG[:, kbase + ch * 512:kbase + (ch + 1) * 512],
                BIG[:, kbase + ch * 512:kbase + (ch + 1) * 512]),
                waits=waits, inc=("V2", 1))
            v_sq = C("V2")
            waits = [("V2", v_sq), ("V2", C("V2"))]
            if ch == 0:
                waits.append(("V", C("V")))
            E("tensor", lambda e, slot=slot: e.matmul(
                BCAST[0:1, 0:512], onesc[:, :], slot,
                start=True, stop=True, skip_group_check=True), waits=waits, inc=("T2", 1))
            if t_prev_norm is None:
                t_prev_norm = {}
            t_prev_norm[ch] = C("T2")
            E("vector", lambda e, ch=ch: e.reduce_max(maxk_list[0:1, ch:ch + 1],
                                                      BCAST[0:1, 0:512], axis=AX.X),
              waits=[("T2", t_prev_norm[ch])], inc=("V2", 1))
        E("vector", lambda e: e.reduce_max(srow[0:1, 0:1], maxk_list[0:1, 0:16], axis=AX.X),
          waits=[("V", v_maxq), ("V2", C("V2"))])
        E("vector", lambda e: e.tensor_add(srow[0:1, 1:2], srow[0:1, 0:1], maxq[0:1, 0:1]))
        E("vector", lambda e: e.tensor_scalar_mul(srow[0:1, 2:3], srow[0:1, 1:2],
                                                  -0.5 * INVSQ))
        E("vector", lambda e: e.tensor_copy(statb[0:1, 0:1], srow[0:1, 2:3]), inc=("V2", 1))
        v_bb = C("V2")
        E("tensor", lambda e: e.matmul(BCAST[:, 0:1], onesr[:, :], statb[0:1, 0:1],
                                       start=True, stop=True, skip_group_check=True),
          waits=[("V2", v_bb)], inc=("T2", 1))
        E("scalar", lambda e, h=h: e.copy(bias_sb[:, h:h + 1], BCAST[:, 0:1]),
          waits=[("T2", C("T2"))], inc=("S2", 1))
        s_bias[h] = C("S2")

    QT3 = QT.ap().rearrange("p (hh s) -> p hh s", s=1024)
    exp_tgt = {}
    pair_tgt = {}
    den_tgt = {}
    v_lrec_prev = [0]
    Kglob = 0

    emit_gather(0, [("T", t_phaseA)])
    emit_gather(1, [("T", t_phaseA)])
    emit_head_prep(0)
    emit_head_prep(1)
    if STAGE == 20:
        dump_and_finish(P, tn, BIG, [("V3", v_ktdone[1]), ("S2", s_bias[1])])
        return

    for h in range(NKV):
        par = h % 2
        kbase = par * 16384
        vbase = par * 16384 + 8192
        for st in range(8):
            qcols = QT3[:, 4 * h:4 * h + 4, st * 128:(st + 1) * 128]
            def emit_scores(pr):
                K = Kglob0 + pr
                cA, cB = 2 * pr, 2 * pr + 1
                sb = [S0, S1][pr % 2]
                waits = []
                if (K - 2) in exp_tgt:
                    waits.append(("SC", exp_tgt[K - 2]))
                if pr == 0:
                    waits += [("V3", v_ktdone[h]), ("S2", s_bias[h])]
                E("tensor", lambda e, sb=sb, kbase=kbase, cA=cA, qcols=qcols: e.matmul(
                    sb[:, 0:512], BIG[:, kbase + cA * 128:kbase + (cA + 1) * 128], qcols,
                    start=True, stop=True, skip_group_check=True), waits=waits)
                E("tensor", lambda e, sb=sb, kbase=kbase, cB=cB, qcols=qcols: e.matmul(
                    sb[:, 512:1024], BIG[:, kbase + cB * 128:kbase + (cB + 1) * 128], qcols,
                    start=True, stop=True, skip_group_check=True), inc=("T", 1))
                t_sc = C("T")
                waits = [("T", t_sc)]
                if (K - 2) in pair_tgt:
                    waits.append(("T", pair_tgt[K - 2]))
                ptv = MISC[:, (pr % 2) * 1024:(pr % 2 + 1) * 1024]
                E("scalar", lambda e, ptv=ptv, sb=sb, h=h: e.activation(
                    ptv, sb[:, :], AF.Exp, bias=bias_sb[:, h:h + 1], scale=INVSQ),
                    waits=waits, inc=("SC", 1))
                exp_tgt[K] = C("SC")

            def emit_ctxden(pr):
                K = Kglob0 + pr
                cA, cB = 2 * pr, 2 * pr + 1
                ptv = MISC[:, (pr % 2) * 1024:(pr % 2 + 1) * 1024]
                waits = [("SC", exp_tgt[K])]
                if pr == 0 and (h, st) != (0, 0):
                    waits.append(("S2", C("S2")))
                    waits.append(("V", C("V")))
                if pr == 0 and (h, st) == (0, 0):
                    waits.append(("V", v_qt_done))
                E("tensor", lambda e, vbase=vbase, cA=cA, ptv=ptv, pr=pr: e.matmul(
                    ACC[:, :], BIG[:, vbase + cA * 128:vbase + (cA + 1) * 128], ptv[:, 0:512],
                    start=(pr == 0), stop=False, skip_group_check=True), waits=waits)
                E("tensor", lambda e, vbase=vbase, cB=cB, ptv=ptv, pr=pr: e.matmul(
                    ACC[:, :], BIG[:, vbase + cB * 128:vbase + (cB + 1) * 128], ptv[:, 512:1024],
                    start=False, stop=(pr == 31), skip_group_check=True))
                E("tensor", lambda e, ptv=ptv, pr=pr: e.matmul(
                    ROW[0:1, 0:512], onesc[:, :], ptv[:, 0:512],
                    start=(pr == 0), stop=False, skip_group_check=True))
                E("tensor", lambda e, ptv=ptv, pr=pr: e.matmul(
                    ROW[0:1, 0:512], onesc[:, :], ptv[:, 512:1024],
                    start=False, stop=(pr == 31), skip_group_check=True), inc=("T", 1))
                pair_tgt[K] = C("T")

            Kglob0 = Kglob
            for pr in range(32):
                emit_scores(pr)
                if pr >= 1:
                    emit_ctxden(pr - 1)
            emit_ctxden(31)
            Kglob += 32
            t_group = C("T")
            E("vector", lambda e: e.reciprocal(lrec[0:1, 0:512], ROW[0:1, 0:512]),
              waits=[("T", t_group)])
            E("vector", lambda e: e.tensor_copy(lrecb[0:1, 0:512], lrec[0:1, 0:512]),
              inc=("V", 1))
            v_l = C("V")
            E("tensor", lambda e: e.matmul(BCAST[:, 0:512], onesr[:, :], lrecb[0:1, 0:512],
                                           start=True, stop=True, skip_group_check=True),
              waits=[("V", v_l)], inc=("T2", 1))
            t_bc = C("T2")
            E("scalar", lambda e: e.copy(FSCR[:, 1024:1536], ACC[:, :]),
              waits=[("T", t_group)], inc=("S2", 1))
            s_ce = C("S2")
            ctxe3 = FSCR[:, 1024:1536].rearrange("p (g s) -> p g s", s=128)
            bc3 = BCAST[:, 0:512].rearrange("p (g s) -> p g s", s=128)
            E("vector", lambda e, qcols=qcols, ctxe3=ctxe3, bc3=bc3: e.tensor_mul(
                qcols, ctxe3, bc3),
              waits=[("T2", t_bc), ("S2", s_ce)], inc=("V", 1))
        t_attn_end[h] = C("T")
        if h + 2 < NKV:
            emit_gather(h + 2)
            emit_head_prep(h + 2)
    v_attn_done = C("V")
    sc_attn_done = C("SC")

    if STAGE == 2:
        dump_and_finish(P, tn, QT, [("V", v_attn_done)])
        return

    wo3 = wo.rearrange("(c p) q -> p c q", p=128)
    t_woslabfree = {}
    s_woevac = {}
    widx = 0
    for do in range(32):
        buf = do % 2
        waits = [("T", t_woslabfree[do - 2])] if do >= 2 else []
        E("gpsimd", lambda e, do=do, buf=buf: e.dma_start(
            out=SLAB[:, buf * 4096:(buf + 1) * 4096].rearrange("p (c q) -> p c q", q=128),
            in_=wo3[:, :, do * 128:(do + 1) * 128]), waits=waits, inc=("G", 16))
        g_woslab = C("G")
        for st2 in range(2):
            bank = [ACC, BCAST][widx % 2]
            for dk in range(32):
                waits = []
                if dk == 0:
                    waits.append(("G", g_woslab))
                    if widx == 0:
                        waits.append(("V", v_attn_done))
                    if widx >= 2:
                        waits.append(("S2", s_woevac[widx - 2]))
                E("tensor", lambda e, bank=bank, buf=buf, dk=dk, st2=st2: e.matmul(
                    bank[:, :], SLAB[:, buf * 4096 + dk * 128:buf * 4096 + (dk + 1) * 128],
                    QT[:, dk * 1024 + st2 * 512: dk * 1024 + st2 * 512 + 512],
                    start=(dk == 0), stop=(dk == 31), skip_group_check=True),
                    waits=waits, inc=("T", 1) if dk == 31 else None)
            t_w = C("T")
            if st2 == 1:
                t_woslabfree[do] = t_w
            E("scalar", lambda e, bank=bank, do=do, st2=st2: e.copy(
                BIG[:, do * 1024 + st2 * 512: do * 1024 + st2 * 512 + 512], bank[:, :]),
                waits=[("T", t_w)], inc=("S2", 1))
            s_woevac[widx] = C("S2")
            widx += 1
    s_wo_done = C("S2")

    t_stats = {}
    t_statmm = {}
    for st2 in range(2):
        ps = [S0, S1][st2]
        for c in range(32):
            slot = FSCR[:, 2048 + (c % 2) * 512:2048 + (c % 2) * 512 + 512]
            waits = [] if (st2 == 0 and c < 2) else [("V", C("V"))]
            E("sync", lambda e, c=c, st2=st2, slot=slot: e.dma_start(
                out=slot, in_=hsT[c * 128:(c + 1) * 128, st2 * 512:(st2 + 1) * 512]),
                waits=waits, inc=("S", 16))
            s_x0 = C("S")
            k6 = st2 * 32 + c
            sq = MISC[:, 2560 + (k6 % 2) * 512: 2560 + (k6 % 2) * 512 + 512]
            xcols = BIG[:, c * 1024 + st2 * 512: c * 1024 + st2 * 512 + 512]
            E("vector", lambda e, xcols=xcols, slot=slot: e.tensor_add(
                xcols, slot, xcols),
                waits=[("S", s_x0), ("S2", s_woevac[c * 2 + st2])])
            waits = [] if k6 < 2 else [("T", t_statmm[k6 - 2])]
            E("vector", lambda e, xcols=xcols, sq=sq: e.tensor_mul(sq, xcols, xcols),
              waits=waits, inc=("V", 1))
            v_x = C("V")
            waits = [("V", v_x)]
            if st2 == 0 and c == 0:
                waits.append(("SC", sc_attn_done))
            E("tensor", lambda e, ps=ps, xcols=xcols, c=c: e.matmul(
                ps[0:1, 0:512], onesc[:, :], xcols,
                start=(c == 0), stop=(c == 31), skip_group_check=True), waits=waits)
            E("tensor", lambda e, ps=ps, c=c, sq=sq: e.matmul(
                ps[0:1, 512:1024], onesc[:, :], sq,
                start=(c == 0), stop=(c == 31), skip_group_check=True), inc=("T", 1))
            t_statmm[k6] = C("T")
        t_stats[st2] = C("T")

    def emit_stats_final(t_wait, rb_banks, mb_banks, statoff):
        t_bcast = {}
        for st2 in range(2):
            ps = [S0, S1][st2]
            E("vector", lambda e, ps=ps, st2=st2, statoff=statoff: e.tensor_copy(
                srow[0:1, statoff + st2 * 1024: statoff + st2 * 1024 + 1024], ps[0:1, 0:1024]),
                waits=[("T", t_wait[st2])], inc=("V", 1))
        v_se = C("V")
        for st2 in range(2):
            o = statoff + st2 * 1024
            E("vector", lambda e, o=o: e.tensor_scalar_mul(
                statf[0:1, 0:512], srow[0:1, o:o + 512], 1.0 / 4096.0), waits=[("V", v_se)])
            E("vector", lambda e, o=o: e.tensor_scalar_mul(
                statf[0:1, 512:1024], srow[0:1, o + 512:o + 1024], 1.0 / 4096.0))
            E("vector", lambda e: e.tensor_mul(statf[0:1, 1024:1536],
                                               statf[0:1, 0:512], statf[0:1, 0:512]))
            E("vector", lambda e: e.tensor_sub(statf[0:1, 512:1024],
                                               statf[0:1, 512:1024], statf[0:1, 1024:1536]),
              inc=("V", 1))
            v_var = C("V")
            E("scalar", lambda e: e.activation(statf[0:1, 1536:2048], statf[0:1, 512:1024],
                                               AF.Ln, bias=np.float32(1e-5)), waits=[("V", v_var)])
            E("scalar", lambda e: e.activation(statf[0:1, 2048:2560], statf[0:1, 1536:2048],
                                               AF.Exp, scale=-0.5), inc=("S2", 1))
            s_rs = C("S2")
            E("vector", lambda e: e.tensor_mul(statf[0:1, 1536:2048], statf[0:1, 0:512],
                                               statf[0:1, 2048:2560]), waits=[("S2", s_rs)])
            E("vector", lambda e, st2=st2, statoff=statoff: e.tensor_copy(
                statb[0:1, 64 + st2 * 1024: 64 + st2 * 1024 + 512], statf[0:1, 2048:2560]))
            E("vector", lambda e, st2=st2, statoff=statoff: e.tensor_copy(
                statb[0:1, 64 + st2 * 1024 + 512: 64 + st2 * 1024 + 1024],
                statf[0:1, 1536:2048]), inc=("V", 1))
            v_b = C("V")
            E("tensor", lambda e, st2=st2, rb=rb_banks[st2]: e.matmul(
                rb, onesr[:, :], statb[0:1, 64 + st2 * 1024: 64 + st2 * 1024 + 512],
                start=True, stop=True, skip_group_check=True), waits=[("V", v_b)])
            E("tensor", lambda e, st2=st2, mb=mb_banks[st2]: e.matmul(
                mb, onesr[:, :], statb[0:1, 64 + st2 * 1024 + 512: 64 + st2 * 1024 + 1024],
                start=True, stop=True, skip_group_check=True), inc=("T2", 1))
            t_bcast[st2] = C("T2")
        return t_bcast

    rb_banks = {0: ACC[:, :], 1: S0[:, 0:512]}
    mb_banks = {0: BCAST[:, :], 1: S0[:, 512:1024]}
    t_bcast = emit_stats_final(t_stats, rb_banks, mb_banks, 0)

    v_x1 = {}
    for st2 in range(2):
        for c in range(32):
            xbc = BIG[:, c * 1024 + st2 * 512: c * 1024 + st2 * 512 + 512]
            xcols = QT[:, c * 1024 + st2 * 512: c * 1024 + st2 * 512 + 512]
            t0 = FSCR[:, 1536:2048]
            E("vector", lambda e, xbc=xbc, rb=rb_banks[st2], t0=t0: e.tensor_mul(
                t0, xbc, rb), waits=[("T2", t_bcast[st2])])
            E("vector", lambda e, xcols=xcols, mb=mb_banks[st2], t0=t0: e.tensor_sub(
                xcols, t0, mb), inc=("V", 1))
            v_x1[(st2, c)] = C("V")
            waits = [("V", v_x1[(st2, c)])]
            E("tensor", lambda e, c=c, xcols=xcols: e.matmul(
                ROW[0:2, 0:512], a1t[:, c * 2:c * 2 + 2], xcols,
                start=(c == 0), stop=(c == 31), skip_group_check=True), waits=waits)
            E("tensor", lambda e, c=c, xcols=xcols: e.matmul(
                S1[0:2, 0:512], a3t[:, c * 2:c * 2 + 2], xcols,
                start=(c == 0), stop=(c == 31), skip_group_check=True),
                inc=("T", 1) if c == 31 else None)
        t_u = C("T")
        E("vector", lambda e, st2=st2: e.tensor_scalar_mul(
            u1b[0:2, st2 * 512:(st2 + 1) * 512], ROW[0:2, 0:512], LORA_SCALE),
            waits=[("T", t_u)])
        E("vector", lambda e, st2=st2: e.tensor_scalar_mul(
            u3b[0:2, st2 * 512:(st2 + 1) * 512], S1[0:2, 0:512], LORA_SCALE), inc=("V", 1))
    v_ln1_done = C("V")

    for st2 in range(2):
        for j in range(8):
            sl = [QT[:, cc * 1024 + st2 * 512: cc * 1024 + st2 * 512 + 512]
                  for cc in (j, j + 8, j + 16, j + 24)]
            dst = BIG[:, 22528 + j * 1024 + st2 * 512: 22528 + j * 1024 + st2 * 512 + 512]
            E("vector", lambda e, sl=sl: e.tensor_add(FSCR[:, 2048:2560], sl[0], sl[1]),
              waits=[("V", v_ln1_done)])
            E("vector", lambda e, sl=sl: e.tensor_add(FSCR[:, 2560:3072], sl[2], sl[3]))
            E("vector", lambda e, dst=dst: e.tensor_add(
                dst, FSCR[:, 2048:2560], FSCR[:, 2560:3072]), inc=("V", 1))
    v_xf_done = C("V")

    if STAGE == 3:
        dump_and_finish(P, tn, QT, [("V", v_xf_done)])
        return

    w1q3 = w1qT.rearrange("(kc p) o -> p kc o", p=128)
    w3q3 = w3qT.rearrange("(kc p) o -> p kc o", p=128)
    oc_list = []
    for i in range(43):
        oc_list += [i, i + 43]
    t_fslabfree = {}
    v_y = {}
    t_zrot = {0: 0, 1: 0}
    for n, oc in enumerate(oc_list):
        buf = n % 2
        i = n // 2
        sel = n - 2 * i
        waits = [("V", v_xf_done)] if n < 2 else [("T", t_fslabfree[oc_list[n - 2]])]
        E("gpsimd", lambda e, oc=oc, buf=buf: e.dma_start(
            out=SLAB[:, buf * 2048:buf * 2048 + 1024].rearrange("p (kc o) -> p kc o", o=128),
            in_=w1q3[:, :, oc * 128:(oc + 1) * 128]), waits=waits, inc=("G", 16))
        E("gpsimd", lambda e, oc=oc, buf=buf: e.dma_start(
            out=SLAB[:, buf * 2048 + 1024:buf * 2048 + 2048].rearrange("p (kc o) -> p kc o", o=128),
            in_=w3q3[:, :, oc * 128:(oc + 1) * 128]), inc=("G", 16))
        E("gpsimd", lambda e, oc=oc, buf=buf: e.dma_start(
            out=b1t[0:2, buf * 128:(buf + 1) * 128], in_=tn["b1Th"][:, oc * 128:(oc + 1) * 128]),
            inc=("G", 16))
        E("gpsimd", lambda e, oc=oc, buf=buf: e.dma_start(
            out=b3t[0:2, buf * 128:(buf + 1) * 128], in_=tn["b3Th"][:, oc * 128:(oc + 1) * 128]),
            inc=("G", 16))
        g_fslab = C("G")
        for st2 in range(2):
            gb = [S0[:, 0:512], S1[:, 0:512]][n % 2]
            ub = [S0[:, 512:1024], S1[:, 512:1024]][n % 2]
            for k in range(8):
                waits = []
                if k == 0:
                    waits.append(("G", g_fslab))
                    if n >= 2 and st2 == 0:
                        waits.append(("V", v_y[(oc_list[n - 2], 0)]))
                        waits.append(("V", v_y[(oc_list[n - 2], 1)]))
                E("tensor", lambda e, gb=gb, buf=buf, k=k, st2=st2: e.matmul(
                    gb, SLAB[:, buf * 2048 + k * 128:buf * 2048 + (k + 1) * 128],
                    BIG[:, 22528 + k * 1024 + st2 * 512: 22528 + k * 1024 + st2 * 512 + 512],
                    start=(k == 0), stop=False, skip_group_check=True), waits=waits)
            E("tensor", lambda e, gb=gb, buf=buf, st2=st2: e.matmul(
                gb, b1t[0:2, buf * 128:(buf + 1) * 128], u1b[0:2, st2 * 512:(st2 + 1) * 512],
                start=False, stop=True, skip_group_check=True), inc=("T", 1))
            t_g = C("T")
            for k in range(8):
                E("tensor", lambda e, ub=ub, buf=buf, k=k, st2=st2: e.matmul(
                    ub, SLAB[:, buf * 2048 + 1024 + k * 128:buf * 2048 + 1024 + (k + 1) * 128],
                    BIG[:, 22528 + k * 1024 + st2 * 512: 22528 + k * 1024 + st2 * 512 + 512],
                    start=(k == 0), stop=False, skip_group_check=True))
            E("tensor", lambda e, ub=ub, buf=buf, st2=st2: e.matmul(
                ub, b3t[0:2, buf * 128:(buf + 1) * 128], u3b[0:2, st2 * 512:(st2 + 1) * 512],
                start=False, stop=True, skip_group_check=True), inc=("T", 1))
            t_up = C("T")
            if st2 == 1:
                t_fslabfree[oc] = t_up
            E("scalar", lambda e, gb=gb: e.activation(MISC[:, 2048:2560], gb, AF.Silu),
              waits=[("T", t_g)], inc=("SC", 1))
            s_sil = C("SC")
            ysl = BIG[:, 30720 + sel * 1024 + st2 * 512: 30720 + sel * 1024 + st2 * 512 + 512]
            E("vector", lambda e, ub=ub, ysl=ysl: e.tensor_mul(ysl, MISC[:, 2048:2560], ub),
              waits=[("SC", s_sil), ("T", t_up)], inc=("V", 1))
            v_y[(oc, st2)] = C("V")
            u2bank = [ROW[0:2, 0:512], BCAST[0:2, 0:512]][st2]
            E("tensor", lambda e, oc=oc, ysl=ysl, u2bank=u2bank, n=n: e.matmul(
                u2bank, a2t[:, oc * 2:oc * 2 + 2], ysl,
                start=(n == 0), stop=(n == len(oc_list) - 1), skip_group_check=True),
                waits=[("V", v_y[(oc, st2)])], inc=("T2", 1))
        if sel == 1:
            for st2 in range(2):
                yA = BIG[:, 30720 + st2 * 512: 30720 + st2 * 512 + 512]
                yB = BIG[:, 31744 + st2 * 512: 31744 + st2 * 512 + 512]
                w = [("V", v_y[(i, st2)]), ("V", v_y[(i + 43, st2)])]
                if i <= 20:
                    yfj = BIG[:, i * 1024 + st2 * 512: i * 1024 + st2 * 512 + 512]
                    E("vector", lambda e, yfj=yfj, yA=yA, yB=yB: e.tensor_add(
                        yfj, yA, yB), waits=w, inc=("V", 1))
                else:
                    zsl = MISC[:, 3072 + st2 * 512: 3072 + st2 * 512 + 512]
                    ph = ACC[:, :]
                    ww = list(w)
                    if i >= 22:
                        ww.append(("T2", t_zrot[st2]))
                    E("vector", lambda e, zsl=zsl, yA=yA, yB=yB: e.tensor_add(
                        zsl, yA, yB), waits=ww, inc=("V", 1))
                    v_z = C("V")
                    if i == 21:
                        yfj = BIG[0:64, 21 * 1024 + st2 * 512: 21 * 1024 + st2 * 512 + 512]
                        E("vector", lambda e, yfj=yfj, zsl=zsl: e.tensor_copy(
                            yfj, zsl[0:64, :]), inc=("V", 1))
                    E("tensor", lambda e, ph=ph, zsl=zsl: e.matmul(
                        ph, perm[:, 128:256], zsl, start=True, stop=True,
                        skip_group_check=True), waits=[("V", v_z), ("V", C("V"))],
                        inc=("T2", 1))
                    t_zr = C("T2")
                    t_zrot[st2] = t_zr
                    j = i - 21
                    yfj = BIG[0:64, j * 1024 + st2 * 512: j * 1024 + st2 * 512 + 512]
                    E("vector", lambda e, yfj=yfj, ph=ph: e.tensor_add(
                        yfj, yfj, ph[0:64, :]), waits=[("T2", t_zr)])
                    if i >= 22:
                        j2 = i - 22
                        yfj2 = BIG[64:128, j2 * 1024 + st2 * 512: j2 * 1024 + st2 * 512 + 512]
                        E("vector", lambda e, yfj2=yfj2, ph=ph: e.tensor_add(
                            yfj2, yfj2, ph[64:128, :]), inc=("V", 1))
                    else:
                        E("vector", lambda e: e.tensor_copy(lrecb[0:1, 0:512],
                                                            lrecb[0:1, 0:512]), inc=("V", 1))
    t_u2_done = C("T2")
    v_yf_done = C("V")
    E("vector", lambda e: e.tensor_scalar_mul(u2b[0:2, 0:512], ROW[0:2, 0:512], LORA_SCALE),
      waits=[("T2", t_u2_done)])
    E("vector", lambda e: e.tensor_scalar_mul(u2b[0:2, 512:1024], BCAST[0:2, 0:512],
                                              LORA_SCALE), inc=("V", 1))
    v_u2 = C("V")

    if STAGE == 4:
        dump_and_finish(P, tn, BIG, [("V", v_yf_done), ("V", v_u2)])
        return

    w2q3a = w2qT[0:2688, :].rearrange("(kc p) o -> p kc o", p=128)
    t_w2free = {}
    v_xb2 = {}
    t_stats2 = {}
    t_statmm2 = {}
    fidx = 0
    for o2 in range(32):
        buf = o2 % 2
        waits = [("V", v_u2)] if o2 < 2 else [("T", t_w2free[o2 - 2])]
        E("gpsimd", lambda e, o2=o2, buf=buf: e.dma_start(
            out=SLAB[:, buf * 2816:buf * 2816 + 2688].rearrange("p (kc o) -> p kc o", o=128),
            in_=w2q3a[:, :, o2 * 128:(o2 + 1) * 128]), waits=waits, inc=("G", 16))
        E("gpsimd", lambda e, o2=o2, buf=buf: e.dma_start(
            out=SLAB[0:64, buf * 2816 + 2688:buf * 2816 + 2816],
            in_=w2qT[2688:2752, o2 * 128:(o2 + 1) * 128]), inc=("G", 16))
        E("gpsimd", lambda e, o2=o2, buf=buf: e.dma_start(
            out=b2t[0:2, buf * 128:(buf + 1) * 128], in_=tn["b2Th"][:, o2 * 128:(o2 + 1) * 128]),
            inc=("G", 16))
        g_w2slab = C("G")
        for st2 in range(2):
            bank = [ACC, BCAST][fidx % 2]
            for k2 in range(21):
                waits = []
                if k2 == 0:
                    waits.append(("G", g_w2slab))
                    if fidx < 2:
                        waits.append(("V", v_yf_done))
                        waits.append(("V", v_u2))
                    if fidx >= 2:
                        waits.append(("V", v_xb2[fidx - 2]))
                E("tensor", lambda e, bank=bank, buf=buf, k2=k2, st2=st2: e.matmul(
                    bank[:, :], SLAB[:, buf * 2816 + k2 * 128:buf * 2816 + (k2 + 1) * 128],
                    BIG[:, k2 * 1024 + st2 * 512: k2 * 1024 + st2 * 512 + 512],
                    start=(k2 == 0), stop=False, skip_group_check=True), waits=waits)
            E("tensor", lambda e, bank=bank, buf=buf, st2=st2: e.matmul(
                bank[:, :], SLAB[0:64, buf * 2816 + 2688:buf * 2816 + 2816],
                BIG[0:64, 21 * 1024 + st2 * 512: 21 * 1024 + st2 * 512 + 512],
                start=False, stop=False, skip_group_check=True))
            E("tensor", lambda e, bank=bank, buf=buf, st2=st2: e.matmul(
                bank[:, :], b2t[0:2, buf * 128:(buf + 1) * 128],
                u2b[0:2, st2 * 512:(st2 + 1) * 512],
                start=False, stop=True, skip_group_check=True), inc=("T", 1))
            t_f = C("T")
            if st2 == 1:
                t_w2free[o2] = t_f
            xcols = QT[:, o2 * 1024 + st2 * 512: o2 * 1024 + st2 * 512 + 512]
            sq = MISC[:, 2560 + (fidx % 2) * 512: 2560 + (fidx % 2) * 512 + 512]
            E("vector", lambda e, xcols=xcols, bank=bank: e.tensor_add(xcols, xcols, bank[:, :]),
              waits=[("T", t_f)])
            waits = [] if fidx < 2 else [("T", t_statmm2[fidx - 2])]
            E("vector", lambda e, xcols=xcols, sq=sq: e.tensor_mul(sq, xcols, xcols),
              waits=waits, inc=("V", 1))
            v_xb2[fidx] = C("V")
            ps = [S0, S1][st2]
            E("tensor", lambda e, ps=ps, xcols=xcols, o2=o2: e.matmul(
                ps[0:1, 0:512], onesc[:, :], xcols,
                start=(o2 == 0), stop=(o2 == 31), skip_group_check=True),
                waits=[("V", v_xb2[fidx])])
            E("tensor", lambda e, ps=ps, o2=o2, sq=sq: e.matmul(
                ps[0:1, 512:1024], onesc[:, :], sq,
                start=(o2 == 0), stop=(o2 == 31), skip_group_check=True), inc=("T", 1))
            t_statmm2[fidx] = C("T")
            if o2 == 31:
                t_stats2[st2] = C("T")
            fidx += 1

    rb_banks2 = {0: ACC[:, :], 1: S0[:, 0:512]}
    mb_banks2 = {0: BCAST[:, :], 1: S0[:, 512:1024]}
    t_bcast2 = emit_stats_final(t_stats2, rb_banks2, mb_banks2, 0)

    oidx = 0
    for st2 in range(2):
        for c in range(32):
            xcols = QT[:, c * 1024 + st2 * 512: c * 1024 + st2 * 512 + 512]
            oslot = FSCR[:, 2048 + (oidx % 2) * 512: 2048 + (oidx % 2) * 512 + 512]
            waits = [("T2", t_bcast2[st2])]
            if oidx >= 2:
                waits.append(("O", (oidx - 1) * 16))
            E("vector", lambda e, xcols=xcols, rb=rb_banks2[st2]: e.tensor_mul(
                FSCR[:, 1536:2048], xcols, rb), waits=waits)
            E("vector", lambda e, oslot=oslot, mb=mb_banks2[st2]: e.tensor_sub(
                oslot, FSCR[:, 1536:2048], mb), inc=("V", 1))
            v_o = C("V")
            E("sync", lambda e, c=c, st2=st2, oslot=oslot: e.dma_start(
                out=outT[c * 128:(c + 1) * 128, st2 * 512:(st2 + 1) * 512], in_=oslot),
                waits=[("V", v_o)], inc=("O", 16))
            oidx += 1
    E("sync", None, waits=[("O", C("O"))])


def dump_and_finish(P, tn, src, waits):
    E = P.emit
    outT = tn["outT"]
    out3 = outT.rearrange("(c p) s -> p c s", p=128)
    src3 = src.ap().rearrange("p (c s) -> p c s", s=1024)
    for q in range(4):
        E("gpsimd", lambda e, q=q: e.dma_start(
            out=out3[:, q * 8:(q + 1) * 8, :], in_=src3[:, q * 8:(q + 1) * 8, :]),
            waits=waits if q == 0 else (), inc=("O", 16))
    E("sync", None, waits=[("O", P.cnt("O"))])


def build_nc():
    nc = bass.Bass("TRN2", target_bir_lowering=False)
    tn = {}
    tn["hsT"] = nc.dram_tensor("hsT", [D, SC], F32, kind="ExternalInput").ap()
    tn["wq"] = nc.dram_tensor("wq", [D, D], F32, kind="ExternalInput").ap()
    tn["wo"] = nc.dram_tensor("wo", [D, D], F32, kind="ExternalInput").ap()
    tn["kvc"] = nc.dram_tensor("kvc", [1024, 2, NKV, 16, HD], F32, kind="ExternalInput").ap()
    tn["bt"] = nc.dram_tensor("bt", [512], I32, kind="ExternalInput").ap()
    tn["w1qT"] = nc.dram_tensor("w1qT", [1024, HF], F32, kind="ExternalInput").ap()
    tn["w3qT"] = nc.dram_tensor("w3qT", [1024, HF], F32, kind="ExternalInput").ap()
    tn["w2qT"] = nc.dram_tensor("w2qT", [KF2, D], F32, kind="ExternalInput").ap()
    tn["a1th"] = nc.dram_tensor("a1th", [128, 64], F32, kind="ExternalInput").ap()
    tn["a3th"] = nc.dram_tensor("a3th", [128, 64], F32, kind="ExternalInput").ap()
    tn["a2th"] = nc.dram_tensor("a2th", [128, 172], F32, kind="ExternalInput").ap()
    tn["b1Th"] = nc.dram_tensor("b1Th", [2, HF], F32, kind="ExternalInput").ap()
    tn["b3Th"] = nc.dram_tensor("b3Th", [2, HF], F32, kind="ExternalInput").ap()
    tn["b2Th"] = nc.dram_tensor("b2Th", [2, D], F32, kind="ExternalInput").ap()
    tn["rc_h"] = nc.dram_tensor("rc_h", [128, SC], BF16, kind="ExternalInput").ap()
    tn["rs_h"] = nc.dram_tensor("rs_h", [128, SC], BF16, kind="ExternalInput").ap()
    tn["onesc_h"] = nc.dram_tensor("onesc_h", [128, 1], BF16, kind="ExternalInput").ap()
    tn["onesr_h"] = nc.dram_tensor("onesr_h", [1, 128], BF16, kind="ExternalInput").ap()
    tn["ident_h"] = nc.dram_tensor("ident_h", [128, 128], BF16, kind="ExternalInput").ap()
    tn["perm_h"] = nc.dram_tensor("perm_h", [128, 256], BF16, kind="ExternalInput").ap()
    tn["outT"] = nc.dram_tensor("outT", [D, SC], F32, kind="ExternalOutput").ap()

    with ExitStack() as ctx:
        sbt = lambda name, shape, dt: ctx.enter_context(nc.sbuf_tensor(name, shape, dt))
        tn["QT"] = sbt("QT", [128, 32768], BF16)
        tn["BIG"] = sbt("BIG", [128, 32768], BF16)
        tn["SLAB"] = sbt("SLAB", [128, 8192], BF16)
        tn["MISC"] = sbt("MISC", [128, 4096], BF16)
        tn["FSCR"] = sbt("FSCR", [128, 3072], F32)
        tn["cos"] = sbt("cos", [128, SC], BF16)
        tn["sin"] = sbt("sin", [128, SC], BF16)
        tn["onesc"] = sbt("onesc", [128, 1], BF16)
        tn["onesr"] = sbt("onesr", [1, 128], BF16)
        tn["ident"] = sbt("ident", [128, 128], BF16)
        tn["perm"] = sbt("perm", [128, 256], BF16)
        tn["idx_sb"] = sbt("idx_sb", [128, 4], I32)
        tn["bias_sb"] = sbt("bias_sb", [128, 8], F32)
        tn["maxq_list"] = sbt("maxq_list", [1, 64], F32)
        tn["maxk_list"] = sbt("maxk_list", [1, 16], F32)
        tn["maxq"] = sbt("maxq", [1, 1], F32)
        tn["srow"] = sbt("srow", [1, 2048], F32)
        tn["statf"] = sbt("statf", [1, 2560], F32)
        tn["statb"] = sbt("statb", [1, 2112], BF16)
        tn["lrec"] = sbt("lrec", [1, 512], F32)
        tn["dena"] = sbt("dena", [128, 512], F32)
        tn["denb"] = sbt("denb", [128, 512], BF16)
        tn["lrecb"] = sbt("lrecb", [1, 512], BF16)
        tn["a1t"] = sbt("a1t", [128, 64], BF16)
        tn["a3t"] = sbt("a3t", [128, 64], BF16)
        tn["a2t"] = sbt("a2t", [128, 172], BF16)
        tn["b1t"] = sbt("b1t", [2, 256], BF16)
        tn["b3t"] = sbt("b3t", [2, 256], BF16)
        tn["b2t"] = sbt("b2t", [2, 256], BF16)
        tn["u1b"] = sbt("u1b", [2, 1024], BF16)
        tn["u3b"] = sbt("u3b", [2, 1024], BF16)
        tn["u2b"] = sbt("u2b", [2, 1024], BF16)
        pst = lambda name, shape, dt: ctx.enter_context(nc.psum_tensor(name, shape, dt))
        tn["S0"] = pst("S0", [128, 1024], F32)
        tn["S1"] = pst("S1", [128, 1024], F32)
        tn["ACC"] = pst("ACC", [128, 512], F32)
        tn["BCAST"] = pst("BCAST", [128, 512], F32)
        tn["ROW"] = pst("ROW", [2, 512], F32)
        tn["PTB"] = pst("PTB", [128, 1024], BF16)

        semnames = ["G", "G2", "S", "O", "T", "T2", "V", "V2", "V3", "SC", "S2"]
        semh = {n: ctx.enter_context(nc.semaphore(n)) for n in semnames}

        P = Plan()
        build_plan(P, tn)

        block = ctx.enter_context(nc.Block())

        def replay(engname):
            def body(eng):
                for (e, fn, waits, incs) in P.ops:
                    if e != engname:
                        continue
                    for (s, v) in waits:
                        eng.wait_ge(semh[s], v)
                    if fn is not None:
                        inst = fn(eng)
                        for (s, amt) in incs:
                            inst = inst.then_inc(semh[s], amt)
                    else:
                        for (s, amt) in incs:
                            eng.sem_inc(semh[s], amt)
            return body

        block.sync(replay("sync"))
        block.gpsimd(replay("gpsimd"))
        block.tensor(replay("tensor"))
        block.vector(replay("vector"))
        block.scalar(replay("scalar"))

    return nc


def _host_inputs(inputs):
    hs = np.asarray(inputs["L0_attn_hidden_states"], dtype=np.float32)
    hs2 = hs.reshape(2 * 4096, D)
    hsT = np.ascontiguousarray(hs2.T)
    wq = np.ascontiguousarray(np.asarray(inputs["L0_attn_Wq"], dtype=np.float32))
    wo = np.ascontiguousarray(np.asarray(inputs["L0_attn_Wo"], dtype=np.float32))
    kvc = np.ascontiguousarray(np.asarray(inputs["L0_attn_kv_cache"], dtype=np.float32))
    bt = np.ascontiguousarray(np.asarray(inputs["L0_attn_block_table"], dtype=np.int32))
    w1qT = np.ascontiguousarray(np.asarray(inputs["L0_ffn_w1q"], dtype=np.float32).T)
    w3qT = np.ascontiguousarray(np.asarray(inputs["L0_ffn_w3q"], dtype=np.float32).T)
    w2qT = np.ascontiguousarray(np.asarray(inputs["L0_ffn_w2q"], dtype=np.float32).T)
    a1 = np.asarray(inputs["L0_ffn_a1"], dtype=np.float32)
    a3 = np.asarray(inputs["L0_ffn_a3"], dtype=np.float32)
    a2 = np.asarray(inputs["L0_ffn_a2"], dtype=np.float32)
    a1th = np.ascontiguousarray(a1.reshape(2, 32, 128).transpose(2, 1, 0).reshape(128, 64))
    a3th = np.ascontiguousarray(a3.reshape(2, 32, 128).transpose(2, 1, 0).reshape(128, 64))
    a2th = np.ascontiguousarray(a2.reshape(2, 86, 128).transpose(2, 1, 0).reshape(128, 172))
    b1Th = np.ascontiguousarray(np.asarray(inputs["L0_ffn_b1"], dtype=np.float32).T)
    b3Th = np.ascontiguousarray(np.asarray(inputs["L0_ffn_b3"], dtype=np.float32).T)
    b2Th = np.ascontiguousarray(np.asarray(inputs["L0_ffn_b2"], dtype=np.float32).T)

    onesc = np.ones((128, 1), dtype=ml_dtypes.bfloat16)
    pm = np.zeros((128, 128), dtype=np.float32)
    for i in range(64):
        pm[64 + i, i] = -1.0
        pm[i, 64 + i] = 1.0
    sh = np.zeros((128, 128), dtype=np.float32)
    for i in range(64):
        sh[64 + i, i] = 1.0
        sh[i, 64 + i] = 1.0
    perm = np.concatenate([pm, sh], axis=1).astype(ml_dtypes.bfloat16)
    onesr = np.ones((1, 128), dtype=ml_dtypes.bfloat16)
    ident = np.eye(128, dtype=ml_dtypes.bfloat16)

    invf = 10000.0 ** (-(np.arange(64, dtype=np.float64)) / 64.0)
    invf2 = np.concatenate([invf, invf])

    in_maps = []
    for core in range(NCORE):
        pos = (core % 4) * 1024 + np.arange(1024, dtype=np.float64)
        ang = invf2[:, None] * pos[None, :]
        rc = np.cos(ang).astype(ml_dtypes.bfloat16)
        rs = np.sin(ang).astype(ml_dtypes.bfloat16)
        m = {
            "hsT": np.ascontiguousarray(hsT[:, core * SC:(core + 1) * SC]),
            "wq": wq, "wo": wo, "kvc": kvc, "bt": bt,
            "w1qT": w1qT, "w3qT": w3qT, "w2qT": w2qT,
            "a1th": a1th, "a3th": a3th, "a2th": a2th,
            "b1Th": b1Th, "b3Th": b3Th, "b2Th": b2Th,
            "rc_h": rc, "rs_h": rs,
            "onesc_h": onesc, "onesr_h": onesr, "ident_h": ident, "perm_h": perm,
        }
        in_maps.append(m)
    return in_maps


def kernel(**inputs):
    in_maps = _host_inputs(inputs)
    nc = build_nc()
    res = run_bass_kernel_spmd(nc, in_maps, core_ids=list(range(NCORE)))
    outT = np.concatenate([r["outT"] for r in res.results], axis=1)
    return np.ascontiguousarray(outT.T).reshape(2, 4096, D).astype(np.float32)


# version 11
# speedup vs baseline: 1.0834x; 1.0834x over previous
import numpy as np
import ml_dtypes
import concourse.bass as bass
import concourse.mybir as mybir
from concourse.bass_utils import run_bass_kernel_spmd
from contextlib import ExitStack

F32 = mybir.dt.float32
BF16 = mybir.dt.bfloat16
I32 = mybir.dt.int32
AF = mybir.ActivationFunctionType
AX = mybir.AxisListType

D = 4096
SC = 1024
NKV = 8
HD = 128
HF = 11008
KF2 = 2752
NCORE = 8
INVSQ = 0.08838834764831843
LORA_SCALE = 0.5

STAGE = 5


class Plan:
    def __init__(self):
        self.ops = []
        self.sem = {}

    def cnt(self, name):
        return self.sem.get(name, 0)

    def emit(self, eng, fn=None, waits=(), inc=None):
        ws = [(s, v) for (s, v) in waits if v > 0]
        incs = []
        if inc is not None:
            for (s, amt) in ([inc] if isinstance(inc, tuple) else inc):
                self.sem[s] = self.sem.get(s, 0) + amt
                incs.append((s, amt))
        self.ops.append((eng, fn, ws, incs))


def build_plan(P, tn):
    E = P.emit
    C = P.cnt
    QT = tn["QT"]; BIG = tn["BIG"]; SLAB = tn["SLAB"]; MISC = tn["MISC"]
    FSCR = tn["FSCR"]
    S0 = tn["S0"]; S1 = tn["S1"]; ACC = tn["ACC"]; BCAST = tn["BCAST"]
    ROW = tn["ROW"]; PTB = tn["PTB"]
    cos = tn["cos"]; sin = tn["sin"]; onesc = tn["onesc"]; onesr = tn["onesr"]
    ident = tn["ident"]; perm = tn["perm"]
    idx_sb = tn["idx_sb"]; bias_sb = tn["bias_sb"]
    maxq_list = tn["maxq_list"]; maxk_list = tn["maxk_list"]
    knsum = tn["knsum"]; kncol = tn["kncol"]; kncolb = tn["kncolb"]
    maxq = tn["maxq"]; srow = tn["srow"]; statf = tn["statf"]; statb = tn["statb"]
    lrec = tn["lrec"]; lrecb = tn["lrecb"]; dena = tn["dena"]; denb = tn["denb"]
    a1t = tn["a1t"]; a3t = tn["a3t"]; a2t = tn["a2t"]
    b1t = tn["b1t"]; b3t = tn["b3t"]; b2t = tn["b2t"]
    u1b = tn["u1b"]; u3b = tn["u3b"]; u2b = tn["u2b"]
    hsT = tn["hsT"]; wq = tn["wq"]; wo = tn["wo"]; kvc = tn["kvc"]; bt = tn["bt"]
    w1qT = tn["w1qT"]; w3qT = tn["w3qT"]; w2qT = tn["w2qT"]
    outT = tn["outT"]

    E("gpsimd", lambda e: e.dma_start(out=idx_sb[:, :], in_=bt.rearrange("(g p) -> p g", p=128),
                                      allow_slow_non_contiguous=True), inc=("G", 16))
    E("gpsimd", lambda e: e.dma_start(out=a1t[:, :], in_=tn["a1th"][:, :]), inc=("G", 16))
    E("gpsimd", lambda e: e.dma_start(out=a3t[:, :], in_=tn["a3th"][:, :]), inc=("G", 16))
    E("gpsimd", lambda e: e.dma_start(out=a2t[:, :], in_=tn["a2th"][:, :]), inc=("G", 16))
    g_consts = C("G")
    E("sync", lambda e: e.dma_start(out=cos[:, :], in_=tn["rc_h"][:, :]), inc=("S", 16))
    E("sync", lambda e: e.dma_start(out=sin[:, :], in_=tn["rs_h"][:, :]), inc=("S", 16))
    E("sync", lambda e: e.dma_start(out=onesc[:, :], in_=tn["onesc_h"][:, :]), inc=("S", 16))
    E("sync", lambda e: e.dma_start(out=onesr[:, :], in_=tn["onesr_h"][:, :]), inc=("S", 16))
    E("sync", lambda e: e.dma_start(out=ident[:, :], in_=tn["ident_h"][:, :]), inc=("S", 16))
    E("sync", lambda e: e.dma_start(out=perm[:, :], in_=tn["perm_h"][:, :]), inc=("S", 16))
    s_consts = C("S")

    for c in range(32):
        E("gpsimd", lambda e, c=c: e.dma_start(out=BIG[:, c * 1024:(c + 1) * 1024],
                                               in_=hsT[c * 128:(c + 1) * 128, :]), inc=("G", 16))
    g_hs = C("G")

    wq3 = wq.rearrange("(c p) q -> p c q", p=128)
    t_slabfree = {}
    v_qdone = {}
    v_qk = {}
    sc_sq = {}
    qn_idx = 0
    for dq in range(32):
        buf = dq % 2
        waits = [("T", t_slabfree[dq - 2])] if dq >= 2 else []
        E("gpsimd", lambda e, dq=dq, buf=buf: e.dma_start(
            out=SLAB[:, buf * 4096:(buf + 1) * 4096].rearrange("p (c q) -> p c q", q=128),
            in_=wq3[:, :, dq * 128:(dq + 1) * 128]), waits=waits, inc=("G", 16))
        g_slab = C("G")
        ps = [S0, S1][buf]
        for st in range(2):
            for dk in range(32):
                waits = []
                if dk == 0:
                    waits.append(("G", g_slab))
                    if dq == 0 and st == 0:
                        waits.append(("G", g_hs))
                    if dq >= 2:
                        waits.append(("V", v_qdone[(dq - 2, st)]))
                        waits.append(("SC", sc_sq[(dq - 2, st)]))
                E("tensor", lambda e, ps=ps, st=st, buf=buf, dk=dk: e.matmul(
                    ps[:, st * 512:(st + 1) * 512],
                    SLAB[:, buf * 4096 + dk * 128: buf * 4096 + (dk + 1) * 128],
                    BIG[:, dk * 1024 + st * 512: dk * 1024 + st * 512 + 512],
                    start=(dk == 0), stop=(dk == 31), skip_group_check=True),
                    waits=waits,
                    inc=("T", 1) if dk == 31 else None)
            t_q = C("T")
            if st == 1:
                t_slabfree[dq] = t_q
            qsl = ps[:, st * 512:(st + 1) * 512]
            cosl = cos[:, st * 512:(st + 1) * 512]
            sinl = sin[:, st * 512:(st + 1) * 512]
            qa = FSCR[:, 0:512]
            qb = FSCR[:, 512:1024]
            base = dq * 1024 + st * 512
            k = dq * 2 + st
            qsb = MISC[:, 3072 + (k % 2) * 512: 3072 + (k % 2) * 512 + 512]
            ph = [ACC, BCAST][k % 2][:, :]
            waits = [("T", t_q)]
            if k >= 2:
                waits.append(("V", v_qk[k - 2]))
            E("scalar", lambda e, qsb=qsb, qsl=qsl: e.copy(qsb, qsl), waits=waits,
              inc=("SC", 1))
            sc_ev = C("SC")
            E("scalar", lambda e, qsl=qsl: e.activation(MISC[:, 2560:3072], qsl, AF.Square),
              inc=("SC", 1))
            sc_sq[(dq, st)] = C("SC")
            waits = [("SC", sc_ev), ("S", s_consts)]
            if k >= 2:
                waits.append(("V", v_qk[k - 2]))
            E("tensor", lambda e, ph=ph, qsb=qsb: e.matmul(
                ph, perm[:, 0:128], qsb, start=True, stop=True, skip_group_check=True),
                waits=waits, inc=("T2", 1))
            t_rot = C("T2")
            E("vector", lambda e, qsb=qsb, cosl=cosl, qa=qa: e.tensor_mul(qa, qsb, cosl),
              waits=[("SC", sc_ev)])
            E("vector", lambda e, ph=ph, sinl=sinl, qb=qb: e.tensor_mul(qb, ph, sinl),
              waits=[("T2", t_rot)])
            E("vector", lambda e, qa=qa, qb=qb, base=base: e.tensor_add(
                QT[:, base:base + 512], qa, qb), inc=("V", 1))
            v_qdone[(dq, st)] = C("V")
            v_qk[k] = C("V")
            waits = [("V", v_qdone[(dq, st)]), ("SC", sc_sq[(dq, st)])]
            if qn_idx >= 1:
                waits.append(("V2", qn_idx))
            E("tensor", lambda e: e.matmul(ROW[0:1, 0:512], onesc[:, :], MISC[:, 2560:3072],
                                           start=True, stop=True, skip_group_check=True),
              waits=waits, inc=("T2", 1))
            t_n = C("T2")
            E("vector", lambda e, i=qn_idx: e.reduce_max(maxq_list[0:1, i:i + 1],
                                                         ROW[0:1, 0:512], axis=AX.X),
              waits=[("T2", t_n)], inc=("V2", 1))
            qn_idx += 1
    E("vector", lambda e: e.reduce_max(maxq[0:1, 0:1], maxq_list[0:1, 0:64], axis=AX.X),
      waits=[("V2", qn_idx)], inc=("V", 1))
    v_maxq = C("V")
    v_qt_done = C("V")
    t_phaseA = C("T")

    if STAGE == 1:
        dump_and_finish(P, tn, QT, [("V", v_qt_done)])
        return

    g_gather = {}
    t_attn_end = {}
    v_ktdone = {}
    s_bias = {}

    def emit_gather(h, extra_waits=()):
        par = h % 2
        waits = [("G", g_consts)] + list(extra_waits)
        if h >= 2:
            waits.append(("T", t_attn_end[h - 2]))
        kvflat = kvc.rearrange("b c hh s d -> b (c hh s d)")
        for cc in range(2):
            eoff = cc * 16384 + h * 2048
            for g in range(4):
                dst = BIG[:, par * 16384 + cc * 8192 + g * 2048:
                          par * 16384 + cc * 8192 + (g + 1) * 2048]
                E("gpsimd", lambda e, dst=dst, g=g, eoff=eoff: e.indirect_dma_start(
                    dst, None, kvflat,
                    bass.IndirectOffsetOnAxis(ap=idx_sb[:, g:g + 1], axis=0),
                    element_offset=eoff),
                    waits=waits if (cc == 0 and g == 0) else (), inc=("G", 16))
        g_gather[h] = C("G")

    def emit_head_prep(h):
        par = h % 2
        kbase = par * 16384
        for q8 in range(8):
            for c8 in range(8):
                cc = q8 * 8 + c8
                waits = []
                if q8 == 0 and c8 == 0:
                    waits.append(("G", g_gather[h]))
                if c8 == 0 and q8 >= 1:
                    waits.append(("V3", C("V3")))
                E("tensor", lambda e, kbase=kbase, cc=cc, c8=c8: e.transpose(
                    PTB[:, c8 * 128:(c8 + 1) * 128],
                    BIG[:, kbase + cc * 128:kbase + (cc + 1) * 128], ident[:, :]),
                    waits=waits, inc=("T", 1) if c8 == 7 else None)
            t_tr = C("T")
            E("vector", lambda e, kbase=kbase, q8=q8: e.tensor_copy(
                BIG[:, kbase + q8 * 1024:kbase + (q8 + 1) * 1024], PTB[:, :]),
                waits=[("T", t_tr)], inc=("V3", 1))
        v_ktdone[h] = C("V3")
        sqs = MISC[:, 2048:4096]
        for g in range(4):
            kgs = BIG[:, kbase + g * 2048: kbase + (g + 1) * 2048]
            waits = [("G", g_gather[h])]
            if h < 2 and g == 0:
                waits += [("T2", C("T2")), ("V2", C("V2"))]
            E("vector", lambda e, sqs=sqs, kgs=kgs: e.tensor_mul(sqs, kgs, kgs), waits=waits)
            E("vector", lambda e, sqs=sqs, g=g: e.tensor_reduce(
                knsum[:, g * 16:g * 16 + 16],
                sqs.rearrange("p (s d) -> p s d", d=128), axis=AX.X,
                op=mybir.AluOpType.add), inc=("V2", 1))
        E("vector", lambda e: e.reduce_max(kncol[:, 0:1], knsum[:, 0:64], axis=AX.X))
        E("vector", lambda e: e.tensor_copy(kncolb[:, 0:1], kncol[:, 0:1]), inc=("V2", 1))
        v_kn = C("V2")
        w2_ = [("V2", v_kn), ("V", C("V"))]
        E("tensor", lambda e: e.matmul(ROW[0:1, 0:128], kncolb[:, 0:1], ident[:, :],
                                       start=True, stop=True, skip_group_check=True),
          waits=w2_, inc=("T2", 1))
        t_kt = C("T2")
        E("vector", lambda e: e.reduce_max(srow[0:1, 0:1], ROW[0:1, 0:128], axis=AX.X),
          waits=[("T2", t_kt), ("V", v_maxq)])
        E("vector", lambda e: e.tensor_add(srow[0:1, 1:2], srow[0:1, 0:1], maxq[0:1, 0:1]))
        E("vector", lambda e: e.tensor_scalar_mul(srow[0:1, 2:3], srow[0:1, 1:2],
                                                  -0.5 * INVSQ))
        E("vector", lambda e: e.tensor_copy(statb[0:1, 0:1], srow[0:1, 2:3]), inc=("V2", 1))
        v_bb = C("V2")
        E("tensor", lambda e: e.matmul(BCAST[:, 0:1], onesr[:, :], statb[0:1, 0:1],
                                       start=True, stop=True, skip_group_check=True),
          waits=[("V2", v_bb), ("V", C("V"))], inc=("T2", 1))
        E("scalar", lambda e, h=h: e.copy(bias_sb[:, h:h + 1], BCAST[:, 0:1]),
          waits=[("T2", C("T2"))], inc=("S2", 1))
        s_bias[h] = C("S2")

    QT3 = QT.ap().rearrange("p (hh s) -> p hh s", s=1024)
    exp_tgt = {}
    pair_tgt = {}
    den_tgt = {}
    den_v = {}
    t_den_prev = [0]
    v_lrec_prev = [0]
    Kglob = 0

    emit_gather(0, [("T", t_phaseA)])
    emit_gather(1, [("T", t_phaseA)])
    emit_head_prep(0)
    emit_head_prep(1)
    if STAGE == 20:
        dump_and_finish(P, tn, BIG, [("V3", v_ktdone[1]), ("S2", s_bias[1])])
        return

    for h in range(NKV):
        par = h % 2
        kbase = par * 16384
        vbase = par * 16384 + 8192
        for st in range(8):
            qcols = QT3[:, 4 * h:4 * h + 4, st * 128:(st + 1) * 128]
            def emit_scores(pr):
                K = Kglob0 + pr
                cA, cB = 2 * pr, 2 * pr + 1
                sb = [S0, S1][pr % 2]
                waits = []
                if (K - 2) in exp_tgt:
                    waits.append(("SC", exp_tgt[K - 2]))
                if pr == 0:
                    waits += [("V3", v_ktdone[h]), ("S2", s_bias[h])]
                E("tensor", lambda e, sb=sb, kbase=kbase, cA=cA, qcols=qcols: e.matmul(
                    sb[:, 0:512], BIG[:, kbase + cA * 128:kbase + (cA + 1) * 128], qcols,
                    start=True, stop=True, skip_group_check=True), waits=waits)
                E("tensor", lambda e, sb=sb, kbase=kbase, cB=cB, qcols=qcols: e.matmul(
                    sb[:, 512:1024], BIG[:, kbase + cB * 128:kbase + (cB + 1) * 128], qcols,
                    start=True, stop=True, skip_group_check=True), inc=("T", 1))
                t_sc = C("T")
                waits = [("T", t_sc)]
                if (K - 2) in pair_tgt:
                    waits.append(("T", pair_tgt[K - 2]))
                    if den_v.get(K - 2) is not None:
                        waits.append(("V", den_v[K - 2]))
                ptv = MISC[:, (pr % 2) * 1024:(pr % 2 + 1) * 1024]
                E("scalar", lambda e, ptv=ptv, sb=sb, h=h: e.activation(
                    ptv, sb[:, :], AF.Exp, bias=bias_sb[:, h:h + 1], scale=INVSQ),
                    waits=waits, inc=("SC", 1))
                exp_tgt[K] = C("SC")

            def emit_ctxden(pr):
                K = Kglob0 + pr
                cA, cB = 2 * pr, 2 * pr + 1
                ptv = MISC[:, (pr % 2) * 1024:(pr % 2 + 1) * 1024]
                waits = [("SC", exp_tgt[K])]
                if pr == 0 and (h, st) != (0, 0):
                    waits.append(("S2", C("S2")))
                    waits.append(("V", C("V")))
                if pr == 0 and (h, st) == (0, 0):
                    waits.append(("V", v_qt_done))
                E("tensor", lambda e, vbase=vbase, cA=cA, ptv=ptv, pr=pr: e.matmul(
                    ACC[:, :], BIG[:, vbase + cA * 128:vbase + (cA + 1) * 128], ptv[:, 0:512],
                    start=(pr == 0), stop=False, skip_group_check=True), waits=waits)
                E("tensor", lambda e, vbase=vbase, cB=cB, ptv=ptv, pr=pr: e.matmul(
                    ACC[:, :], BIG[:, vbase + cB * 128:vbase + (cB + 1) * 128], ptv[:, 512:1024],
                    start=False, stop=(pr == 31), skip_group_check=True), inc=("T", 1))
                if pr % 6 == 0:
                    w2_ = []
                    if pr == 0:
                        w2_.append(("V2", C("V2")))
                    E("tensor", lambda e, ptv=ptv, pr=pr: e.matmul(
                        ROW[0:1, 0:512], onesc[:, :], ptv[:, 0:512],
                        start=(pr == 0), stop=False, skip_group_check=True), waits=w2_)
                    E("tensor", lambda e, ptv=ptv: e.matmul(
                        ROW[0:1, 0:512], onesc[:, :], ptv[:, 512:1024],
                        start=False, stop=False, skip_group_check=True), inc=("T", 1))
                    pair_tgt[K] = C("T")
                    den_v[K] = None
                else:
                    pair_tgt[K] = C("T")
                    dw = [("SC", exp_tgt[K])]
                    if pr == 1:
                        E("vector", lambda e, ptv=ptv: e.tensor_copy(dena[:, :], ptv[:, :]),
                          waits=dw, inc=("V", 1))
                    else:
                        E("vector", lambda e, ptv=ptv: e.tensor_add(dena[:, :], dena[:, :],
                                                                    ptv[:, :]), waits=dw,
                          inc=("V", 1))
                    den_v[K] = C("V")

            Kglob0 = Kglob
            for pr in range(32):
                emit_scores(pr)
                if pr >= 1:
                    emit_ctxden(pr - 1)
            emit_ctxden(31)
            Kglob += 32
            t_group = C("T")
            E("vector", lambda e: e.tensor_add(dena[:, 0:512], dena[:, 0:512],
                                               dena[:, 512:1024]))
            dbw = [("T2", t_den_prev[0])] if t_den_prev[0] else []
            E("vector", lambda e: e.tensor_copy(denb[:, :], dena[:, 0:512]), waits=dbw,
              inc=("V", 1))
            v_db = C("V")
            E("tensor", lambda e: e.matmul(ROW[0:1, 0:512], onesc[:, :], denb[:, :],
                                           start=False, stop=True, skip_group_check=True),
              waits=[("V", v_db), ("V2", C("V2"))], inc=("T2", 1))
            t_den = C("T2")
            t_den_prev[0] = t_den
            E("vector", lambda e: e.reciprocal(lrec[0:1, 0:512], ROW[0:1, 0:512]),
              waits=[("T2", t_den)])
            E("vector", lambda e: e.tensor_copy(lrecb[0:1, 0:512], lrec[0:1, 0:512]),
              inc=("V", 1))
            v_l = C("V")
            E("tensor", lambda e: e.matmul(BCAST[:, 0:512], onesr[:, :], lrecb[0:1, 0:512],
                                           start=True, stop=True, skip_group_check=True),
              waits=[("V", v_l)], inc=("T2", 1))
            t_bc = C("T2")
            E("scalar", lambda e: e.copy(FSCR[:, 1024:1536], ACC[:, :]),
              waits=[("T", t_group)], inc=("S2", 1))
            s_ce = C("S2")
            ctxe3 = FSCR[:, 1024:1536].rearrange("p (g s) -> p g s", s=128)
            bc3 = BCAST[:, 0:512].rearrange("p (g s) -> p g s", s=128)
            E("vector", lambda e, qcols=qcols, ctxe3=ctxe3, bc3=bc3: e.tensor_mul(
                qcols, ctxe3, bc3),
              waits=[("T2", t_bc), ("S2", s_ce)], inc=("V", 1))
        t_attn_end[h] = C("T")
        if h + 2 < NKV:
            emit_gather(h + 2)
            emit_head_prep(h + 2)
    v_attn_done = C("V")
    sc_attn_done = C("SC")

    if STAGE == 2:
        dump_and_finish(P, tn, QT, [("V", v_attn_done)])
        return

    wo3 = wo.rearrange("(c p) q -> p c q", p=128)
    t_woslabfree = {}
    s_woevac = {}
    widx = 0
    for do in range(32):
        buf = do % 2
        waits = [("T", t_woslabfree[do - 2])] if do >= 2 else []
        E("gpsimd", lambda e, do=do, buf=buf: e.dma_start(
            out=SLAB[:, buf * 4096:(buf + 1) * 4096].rearrange("p (c q) -> p c q", q=128),
            in_=wo3[:, :, do * 128:(do + 1) * 128]), waits=waits, inc=("G", 16))
        g_woslab = C("G")
        for st2 in range(2):
            bank = [ACC, BCAST][widx % 2]
            for dk in range(32):
                waits = []
                if dk == 0:
                    waits.append(("G", g_woslab))
                    if widx == 0:
                        waits.append(("V", v_attn_done))
                    if widx >= 2:
                        waits.append(("S2", s_woevac[widx - 2]))
                E("tensor", lambda e, bank=bank, buf=buf, dk=dk, st2=st2: e.matmul(
                    bank[:, :], SLAB[:, buf * 4096 + dk * 128:buf * 4096 + (dk + 1) * 128],
                    QT[:, dk * 1024 + st2 * 512: dk * 1024 + st2 * 512 + 512],
                    start=(dk == 0), stop=(dk == 31), skip_group_check=True),
                    waits=waits, inc=("T", 1) if dk == 31 else None)
            t_w = C("T")
            if st2 == 1:
                t_woslabfree[do] = t_w
            E("scalar", lambda e, bank=bank, do=do, st2=st2: e.copy(
                BIG[:, do * 1024 + st2 * 512: do * 1024 + st2 * 512 + 512], bank[:, :]),
                waits=[("T", t_w)], inc=("S2", 1))
            s_woevac[widx] = C("S2")
            widx += 1
    s_wo_done = C("S2")
    if STAGE == 30:
        dump_and_finish(P, tn, BIG, [("S2", s_wo_done)])
        return
    t_stats = {}
    t_statmm = {}
    t_bcast = {}
    v_x1 = {}
    rb_banks = {0: ACC[:, :], 1: S0[:, 0:512]}
    mb_banks = {0: BCAST[:, :], 1: S0[:, 512:1024]}

    def emit_pass1(st2):
        ps = [S0, S1][st2]
        for c in range(32):
            slot = FSCR[:, 2048 + (c % 2) * 512:2048 + (c % 2) * 512 + 512]
            waits = [] if (st2 == 0 and c < 2) else [("V", C("V"))]
            E("sync", lambda e, c=c, st2=st2, slot=slot: e.dma_start(
                out=slot, in_=hsT[c * 128:(c + 1) * 128, st2 * 512:(st2 + 1) * 512]),
                waits=waits, inc=("S", 16))
            s_x0 = C("S")
            k6 = st2 * 32 + c
            sq = MISC[:, 2560 + (k6 % 2) * 512: 2560 + (k6 % 2) * 512 + 512]
            xcols = BIG[:, c * 1024 + st2 * 512: c * 1024 + st2 * 512 + 512]
            E("vector", lambda e, xcols=xcols, slot=slot: e.tensor_add(
                xcols, slot, xcols),
                waits=[("S", s_x0), ("S2", s_woevac[c * 2 + st2])])
            waits = [] if k6 < 2 else [("T", t_statmm[k6 - 2])]
            E("vector", lambda e, xcols=xcols, sq=sq: e.tensor_mul(sq, xcols, xcols),
              waits=waits, inc=("V", 1))
            v_x = C("V")
            waits = [("V", v_x)]
            if st2 == 0 and c == 0:
                waits.append(("SC", sc_attn_done))
            E("tensor", lambda e, ps=ps, xcols=xcols, c=c: e.matmul(
                ps[0:1, 0:512], onesc[:, :], xcols,
                start=(c == 0), stop=(c == 31), skip_group_check=True), waits=waits)
            E("tensor", lambda e, ps=ps, c=c, sq=sq: e.matmul(
                ps[0:1, 512:1024], onesc[:, :], sq,
                start=(c == 0), stop=(c == 31), skip_group_check=True), inc=("T", 1))
            t_statmm[k6] = C("T")
        t_stats[st2] = C("T")

    def emit_finalize(st2, t_wait, rb, mb, statoff):
        ps = [S0, S1][st2]
        E("vector", lambda e, ps=ps, st2=st2, statoff=statoff: e.tensor_copy(
            srow[0:1, statoff + st2 * 1024: statoff + st2 * 1024 + 1024], ps[0:1, 0:1024]),
            waits=[("T", t_wait)], inc=("V", 1))
        v_se = C("V")
        o = statoff + st2 * 1024
        E("vector", lambda e, o=o: e.tensor_scalar_mul(
            statf[0:1, 0:512], srow[0:1, o:o + 512], 1.0 / 4096.0), waits=[("V", v_se)])
        E("vector", lambda e, o=o: e.tensor_scalar_mul(
            statf[0:1, 512:1024], srow[0:1, o + 512:o + 1024], 1.0 / 4096.0))
        E("vector", lambda e: e.tensor_mul(statf[0:1, 1024:1536],
                                           statf[0:1, 0:512], statf[0:1, 0:512]))
        E("vector", lambda e: e.tensor_sub(statf[0:1, 512:1024],
                                           statf[0:1, 512:1024], statf[0:1, 1024:1536]),
          inc=("V", 1))
        v_var = C("V")
        E("scalar", lambda e: e.activation(statf[0:1, 1536:2048], statf[0:1, 512:1024],
                                           AF.Ln, bias=np.float32(1e-5)), waits=[("V", v_var)])
        E("scalar", lambda e: e.activation(statf[0:1, 1024:1536], statf[0:1, 1536:2048],
                                           AF.Exp, scale=-0.5), inc=("S2", 1))
        s_rs = C("S2")
        E("vector", lambda e: e.tensor_mul(statf[0:1, 1536:2048], statf[0:1, 0:512],
                                           statf[0:1, 1024:1536]), waits=[("S2", s_rs)])
        E("vector", lambda e, st2=st2: e.tensor_copy(
            statb[0:1, 64 + st2 * 1024: 64 + st2 * 1024 + 512], statf[0:1, 1024:1536]))
        E("vector", lambda e, st2=st2: e.tensor_copy(
            statb[0:1, 64 + st2 * 1024 + 512: 64 + st2 * 1024 + 1024],
            statf[0:1, 1536:2048]), inc=("V", 1))
        v_b = C("V")
        E("tensor", lambda e, st2=st2, rb=rb: e.matmul(
            rb, onesr[:, :], statb[0:1, 64 + st2 * 1024: 64 + st2 * 1024 + 512],
            start=True, stop=True, skip_group_check=True), waits=[("V", v_b), ("V", C("V"))])
        E("tensor", lambda e, st2=st2, mb=mb: e.matmul(
            mb, onesr[:, :], statb[0:1, 64 + st2 * 1024 + 512: 64 + st2 * 1024 + 1024],
            start=True, stop=True, skip_group_check=True), inc=("T2", 1))
        return C("T2")

    def emit_pass2(st2):
        for c in range(32):
            xbc = BIG[:, c * 1024 + st2 * 512: c * 1024 + st2 * 512 + 512]
            xcols = QT[:, c * 1024 + st2 * 512: c * 1024 + st2 * 512 + 512]
            t0 = FSCR[:, 1536:2048]
            E("vector", lambda e, xbc=xbc, rb=rb_banks[st2], t0=t0: e.tensor_mul(
                t0, xbc, rb), waits=[("T2", t_bcast[st2])])
            E("vector", lambda e, xcols=xcols, mb=mb_banks[st2], t0=t0: e.tensor_sub(
                xcols, t0, mb), inc=("V", 1))
            v_x1[(st2, c)] = C("V")

    def emit_lora_u(st2):
        for c in range(32):
            xcols = QT[:, c * 1024 + st2 * 512: c * 1024 + st2 * 512 + 512]
            waits = [("V", v_x1[(st2, c)])]
            if c == 0:
                waits.append(("V", C("V")))
            E("tensor", lambda e, c=c, xcols=xcols: e.matmul(
                ROW[0:2, 0:512], a1t[:, c * 2:c * 2 + 2], xcols,
                start=(c == 0), stop=(c == 31), skip_group_check=True), waits=waits)
            E("tensor", lambda e, c=c, xcols=xcols: e.matmul(
                S1[0:2, 0:512], a3t[:, c * 2:c * 2 + 2], xcols,
                start=(c == 0), stop=(c == 31), skip_group_check=True),
                inc=("T", 1) if c == 31 else None)
        t_u = C("T")
        E("vector", lambda e, st2=st2: e.tensor_scalar_mul(
            u1b[0:2, st2 * 512:(st2 + 1) * 512], ROW[0:2, 0:512], LORA_SCALE),
            waits=[("T", t_u)])
        E("vector", lambda e, st2=st2: e.tensor_scalar_mul(
            u3b[0:2, st2 * 512:(st2 + 1) * 512], S1[0:2, 0:512], LORA_SCALE), inc=("V", 1))

    emit_pass1(0)
    t_bcast[0] = emit_finalize(0, t_stats[0], rb_banks[0], mb_banks[0], 0)
    if STAGE == 31:
        dump_and_finish(P, tn, BIG, [("T2", t_bcast[0])])
        return
    emit_pass1(1)
    emit_pass2(0)
    if STAGE == 32:
        dump_and_finish(P, tn, QT, [("V", C("V")), ("T", t_stats[1])])
        return
    t_bcast[1] = emit_finalize(1, t_stats[1], rb_banks[1], mb_banks[1], 0)
    emit_pass2(1)
    if STAGE == 33:
        dump_and_finish(P, tn, QT, [("V", C("V"))])
        return
    emit_lora_u(0)
    emit_lora_u(1)
    v_ln1_done = C("V")

    for st2 in range(2):
        for j in range(8):
            sl = [QT[:, cc * 1024 + st2 * 512: cc * 1024 + st2 * 512 + 512]
                  for cc in (j, j + 8, j + 16, j + 24)]
            dst = BIG[:, 22528 + j * 1024 + st2 * 512: 22528 + j * 1024 + st2 * 512 + 512]
            E("vector", lambda e, sl=sl: e.tensor_add(FSCR[:, 2048:2560], sl[0], sl[1]),
              waits=[("V", v_ln1_done)])
            E("vector", lambda e, sl=sl: e.tensor_add(FSCR[:, 2560:3072], sl[2], sl[3]))
            E("vector", lambda e, dst=dst: e.tensor_add(
                dst, FSCR[:, 2048:2560], FSCR[:, 2560:3072]), inc=("V", 1))
    v_xf_done = C("V")

    if STAGE == 3:
        dump_and_finish(P, tn, QT, [("V", v_xf_done)])
        return

    w1q3 = w1qT.rearrange("(kc p) o -> p kc o", p=128)
    w3q3 = w3qT.rearrange("(kc p) o -> p kc o", p=128)
    oc_list = []
    for i in range(43):
        oc_list += [i, i + 43]
    t_fslabfree = {}
    v_y = {}
    u2_tgt = {}
    pend_u2 = []
    t_zrot = {0: 0, 1: 0}
    for n, oc in enumerate(oc_list):
        buf = n % 2
        i = n // 2
        sel = n - 2 * i
        waits = [("V", v_xf_done)] if n < 2 else [("T", t_fslabfree[oc_list[n - 2]])]
        E("gpsimd", lambda e, oc=oc, buf=buf: e.dma_start(
            out=SLAB[:, buf * 2048:buf * 2048 + 1024].rearrange("p (kc o) -> p kc o", o=128),
            in_=w1q3[:, :, oc * 128:(oc + 1) * 128]), waits=waits, inc=("G", 16))
        E("gpsimd", lambda e, oc=oc, buf=buf: e.dma_start(
            out=SLAB[:, buf * 2048 + 1024:buf * 2048 + 2048].rearrange("p (kc o) -> p kc o", o=128),
            in_=w3q3[:, :, oc * 128:(oc + 1) * 128]), inc=("G", 16))
        E("gpsimd", lambda e, oc=oc, buf=buf: e.dma_start(
            out=b1t[0:2, buf * 128:(buf + 1) * 128], in_=tn["b1Th"][:, oc * 128:(oc + 1) * 128]),
            inc=("G", 16))
        E("gpsimd", lambda e, oc=oc, buf=buf: e.dma_start(
            out=b3t[0:2, buf * 128:(buf + 1) * 128], in_=tn["b3Th"][:, oc * 128:(oc + 1) * 128]),
            inc=("G", 16))
        g_fslab = C("G")
        for (poc, pst2, pysl, pn) in pend_u2:
            u2bank = [ROW[0:2, 0:512], BCAST[0:2, 0:512]][pst2]
            E("tensor", lambda e, poc=poc, pysl=pysl, u2bank=u2bank, pn=pn: e.matmul(
                u2bank, a2t[:, poc * 2:poc * 2 + 2], pysl,
                start=(pn == 0), stop=(pn == len(oc_list) - 1), skip_group_check=True),
                waits=[("V", v_y[(poc, pst2)])], inc=("T2", 1))
            u2_tgt[(poc, pst2)] = C("T2")
        pend_u2 = []
        for st2 in range(2):
            gb = [S0[:, 0:512], S1[:, 0:512]][n % 2]
            ub = [S0[:, 512:1024], S1[:, 512:1024]][n % 2]
            for k in range(8):
                waits = []
                if k == 0:
                    waits.append(("G", g_fslab))
                    if n >= 2 and st2 == 0:
                        waits.append(("V", v_y[(oc_list[n - 2], 0)]))
                        waits.append(("V", v_y[(oc_list[n - 2], 1)]))
                E("tensor", lambda e, gb=gb, buf=buf, k=k, st2=st2: e.matmul(
                    gb, SLAB[:, buf * 2048 + k * 128:buf * 2048 + (k + 1) * 128],
                    BIG[:, 22528 + k * 1024 + st2 * 512: 22528 + k * 1024 + st2 * 512 + 512],
                    start=(k == 0), stop=False, skip_group_check=True), waits=waits)
            E("tensor", lambda e, gb=gb, buf=buf, st2=st2: e.matmul(
                gb, b1t[0:2, buf * 128:(buf + 1) * 128], u1b[0:2, st2 * 512:(st2 + 1) * 512],
                start=False, stop=True, skip_group_check=True), inc=("T", 1))
            t_g = C("T")
            for k in range(8):
                E("tensor", lambda e, ub=ub, buf=buf, k=k, st2=st2: e.matmul(
                    ub, SLAB[:, buf * 2048 + 1024 + k * 128:buf * 2048 + 1024 + (k + 1) * 128],
                    BIG[:, 22528 + k * 1024 + st2 * 512: 22528 + k * 1024 + st2 * 512 + 512],
                    start=(k == 0), stop=False, skip_group_check=True))
            E("tensor", lambda e, ub=ub, buf=buf, st2=st2: e.matmul(
                ub, b3t[0:2, buf * 128:(buf + 1) * 128], u3b[0:2, st2 * 512:(st2 + 1) * 512],
                start=False, stop=True, skip_group_check=True), inc=("T", 1))
            t_up = C("T")
            if st2 == 1:
                t_fslabfree[oc] = t_up
            E("scalar", lambda e, gb=gb: e.activation(MISC[:, 2048:2560], gb, AF.Silu),
              waits=[("T", t_g)], inc=("SC", 1))
            s_sil = C("SC")
            ysl = BIG[:, 30720 + sel * 1024 + st2 * 512: 30720 + sel * 1024 + st2 * 512 + 512]
            yw = [("SC", s_sil), ("T", t_up)]
            if n >= 2 and (oc_list[n - 2], st2) in u2_tgt:
                yw.append(("T2", u2_tgt[(oc_list[n - 2], st2)]))
            E("vector", lambda e, ub=ub, ysl=ysl: e.tensor_mul(ysl, MISC[:, 2048:2560], ub),
              waits=yw, inc=("V", 1))
            v_y[(oc, st2)] = C("V")
            pend_u2.append((oc, st2, ysl, n))
        if sel == 1:
            for st2 in range(2):
                yA = BIG[:, 30720 + st2 * 512: 30720 + st2 * 512 + 512]
                yB = BIG[:, 31744 + st2 * 512: 31744 + st2 * 512 + 512]
                w = [("V", v_y[(i, st2)]), ("V", v_y[(i + 43, st2)])]
                if i <= 20:
                    yfj = BIG[:, i * 1024 + st2 * 512: i * 1024 + st2 * 512 + 512]
                    E("vector", lambda e, yfj=yfj, yA=yA, yB=yB: e.tensor_add(
                        yfj, yA, yB), waits=w, inc=("V", 1))
                else:
                    zsl = MISC[:, 3072 + st2 * 512: 3072 + st2 * 512 + 512]
                    ph = ACC[:, :]
                    ww = list(w)
                    if i >= 22:
                        ww.append(("T2", t_zrot[st2]))
                    E("vector", lambda e, zsl=zsl, yA=yA, yB=yB: e.tensor_add(
                        zsl, yA, yB), waits=ww, inc=("V", 1))
                    v_z = C("V")
                    if i == 21:
                        yfj = BIG[0:64, 21 * 1024 + st2 * 512: 21 * 1024 + st2 * 512 + 512]
                        E("vector", lambda e, yfj=yfj, zsl=zsl: e.tensor_copy(
                            yfj, zsl[0:64, :]), inc=("V", 1))
                    E("tensor", lambda e, ph=ph, zsl=zsl: e.matmul(
                        ph, perm[:, 128:256], zsl, start=True, stop=True,
                        skip_group_check=True), waits=[("V", v_z), ("V", C("V"))],
                        inc=("T2", 1))
                    t_zr = C("T2")
                    t_zrot[st2] = t_zr
                    j = i - 21
                    yfj = BIG[0:64, j * 1024 + st2 * 512: j * 1024 + st2 * 512 + 512]
                    E("vector", lambda e, yfj=yfj, ph=ph: e.tensor_add(
                        yfj, yfj, ph[0:64, :]), waits=[("T2", t_zr)])
                    if i >= 22:
                        j2 = i - 22
                        yfj2 = BIG[64:128, j2 * 1024 + st2 * 512: j2 * 1024 + st2 * 512 + 512]
                        E("vector", lambda e, yfj2=yfj2, ph=ph: e.tensor_add(
                            yfj2, yfj2, ph[64:128, :]), inc=("V", 1))
                    else:
                        E("vector", lambda e: e.tensor_copy(lrecb[0:1, 0:512],
                                                            lrecb[0:1, 0:512]), inc=("V", 1))
    for (poc, pst2, pysl, pn) in pend_u2:
        u2bank = [ROW[0:2, 0:512], BCAST[0:2, 0:512]][pst2]
        E("tensor", lambda e, poc=poc, pysl=pysl, u2bank=u2bank, pn=pn: e.matmul(
            u2bank, a2t[:, poc * 2:poc * 2 + 2], pysl,
            start=(pn == 0), stop=(pn == len(oc_list) - 1), skip_group_check=True),
            waits=[("V", v_y[(poc, pst2)])], inc=("T2", 1))
        u2_tgt[(poc, pst2)] = C("T2")
    pend_u2 = []
    t_u2_done = C("T2")
    v_yf_done = C("V")
    E("vector", lambda e: e.tensor_scalar_mul(u2b[0:2, 0:512], ROW[0:2, 0:512], LORA_SCALE),
      waits=[("T2", t_u2_done)])
    E("vector", lambda e: e.tensor_scalar_mul(u2b[0:2, 512:1024], BCAST[0:2, 0:512],
                                              LORA_SCALE), inc=("V", 1))
    v_u2 = C("V")

    if STAGE == 4:
        dump_and_finish(P, tn, BIG, [("V", v_yf_done), ("V", v_u2)])
        return

    w2q3a = w2qT[0:2688, :].rearrange("(kc p) o -> p kc o", p=128)
    t_w2free = {}
    v_xb2 = {}
    t_stats2 = {}
    t_statmm2 = {}
    fidx = 0
    for o2 in range(32):
        buf = o2 % 2
        waits = [("V", v_u2)] if o2 < 2 else [("T", t_w2free[o2 - 2])]
        E("gpsimd", lambda e, o2=o2, buf=buf: e.dma_start(
            out=SLAB[:, buf * 2816:buf * 2816 + 2688].rearrange("p (kc o) -> p kc o", o=128),
            in_=w2q3a[:, :, o2 * 128:(o2 + 1) * 128]), waits=waits, inc=("G", 16))
        E("gpsimd", lambda e, o2=o2, buf=buf: e.dma_start(
            out=SLAB[0:64, buf * 2816 + 2688:buf * 2816 + 2816],
            in_=w2qT[2688:2752, o2 * 128:(o2 + 1) * 128]), inc=("G", 16))
        E("gpsimd", lambda e, o2=o2, buf=buf: e.dma_start(
            out=b2t[0:2, buf * 128:(buf + 1) * 128], in_=tn["b2Th"][:, o2 * 128:(o2 + 1) * 128]),
            inc=("G", 16))
        g_w2slab = C("G")
        for st2 in range(2):
            bank = [ACC, BCAST][fidx % 2]
            for k2 in range(21):
                waits = []
                if k2 == 0:
                    waits.append(("G", g_w2slab))
                    if fidx < 2:
                        waits.append(("V", v_yf_done))
                        waits.append(("V", v_u2))
                    if fidx >= 2:
                        waits.append(("V", v_xb2[fidx - 2]))
                E("tensor", lambda e, bank=bank, buf=buf, k2=k2, st2=st2: e.matmul(
                    bank[:, :], SLAB[:, buf * 2816 + k2 * 128:buf * 2816 + (k2 + 1) * 128],
                    BIG[:, k2 * 1024 + st2 * 512: k2 * 1024 + st2 * 512 + 512],
                    start=(k2 == 0), stop=False, skip_group_check=True), waits=waits)
            E("tensor", lambda e, bank=bank, buf=buf, st2=st2: e.matmul(
                bank[:, :], SLAB[0:64, buf * 2816 + 2688:buf * 2816 + 2816],
                BIG[0:64, 21 * 1024 + st2 * 512: 21 * 1024 + st2 * 512 + 512],
                start=False, stop=False, skip_group_check=True))
            E("tensor", lambda e, bank=bank, buf=buf, st2=st2: e.matmul(
                bank[:, :], b2t[0:2, buf * 128:(buf + 1) * 128],
                u2b[0:2, st2 * 512:(st2 + 1) * 512],
                start=False, stop=True, skip_group_check=True), inc=("T", 1))
            t_f = C("T")
            if st2 == 1:
                t_w2free[o2] = t_f
            xcols = QT[:, o2 * 1024 + st2 * 512: o2 * 1024 + st2 * 512 + 512]
            sq = MISC[:, 2560 + (fidx % 2) * 512: 2560 + (fidx % 2) * 512 + 512]
            E("vector", lambda e, xcols=xcols, bank=bank: e.tensor_add(xcols, xcols, bank[:, :]),
              waits=[("T", t_f)])
            waits = [] if fidx < 2 else [("T", t_statmm2[fidx - 2])]
            E("vector", lambda e, xcols=xcols, sq=sq: e.tensor_mul(sq, xcols, xcols),
              waits=waits, inc=("V", 1))
            v_xb2[fidx] = C("V")
            ps = [S0, S1][st2]
            E("tensor", lambda e, ps=ps, xcols=xcols, o2=o2: e.matmul(
                ps[0:1, 0:512], onesc[:, :], xcols,
                start=(o2 == 0), stop=(o2 == 31), skip_group_check=True),
                waits=[("V", v_xb2[fidx])])
            E("tensor", lambda e, ps=ps, o2=o2, sq=sq: e.matmul(
                ps[0:1, 512:1024], onesc[:, :], sq,
                start=(o2 == 0), stop=(o2 == 31), skip_group_check=True), inc=("T", 1))
            t_statmm2[fidx] = C("T")
            if o2 == 31:
                t_stats2[st2] = C("T")
            fidx += 1

    rb_banks2 = {0: ACC[:, :], 1: S0[:, 0:512]}
    mb_banks2 = {0: BCAST[:, :], 1: S0[:, 512:1024]}
    t_bcast2 = {}
    t_bcast2[0] = emit_finalize(0, t_stats2[0], rb_banks2[0], mb_banks2[0], 0)
    t_bcast2[1] = emit_finalize(1, t_stats2[1], rb_banks2[1], mb_banks2[1], 0)

    oidx = 0
    for st2 in range(2):
        for c in range(32):
            xcols = QT[:, c * 1024 + st2 * 512: c * 1024 + st2 * 512 + 512]
            oslot = FSCR[:, 2048 + (oidx % 2) * 512: 2048 + (oidx % 2) * 512 + 512]
            waits = [("T2", t_bcast2[st2])]
            if oidx >= 2:
                waits.append(("O", (oidx - 1) * 16))
            E("vector", lambda e, xcols=xcols, rb=rb_banks2[st2]: e.tensor_mul(
                FSCR[:, 1536:2048], xcols, rb), waits=waits)
            E("vector", lambda e, oslot=oslot, mb=mb_banks2[st2]: e.tensor_sub(
                oslot, FSCR[:, 1536:2048], mb), inc=("V", 1))
            v_o = C("V")
            E("sync", lambda e, c=c, st2=st2, oslot=oslot: e.dma_start(
                out=outT[c * 128:(c + 1) * 128, st2 * 512:(st2 + 1) * 512], in_=oslot),
                waits=[("V", v_o)], inc=("O", 16))
            oidx += 1
    E("sync", None, waits=[("O", C("O"))])


def dump_and_finish(P, tn, src, waits):
    E = P.emit
    outT = tn["outT"]
    out3 = outT.rearrange("(c p) s -> p c s", p=128)
    src3 = src.ap().rearrange("p (c s) -> p c s", s=1024)
    for q in range(4):
        E("gpsimd", lambda e, q=q: e.dma_start(
            out=out3[:, q * 8:(q + 1) * 8, :], in_=src3[:, q * 8:(q + 1) * 8, :]),
            waits=waits if q == 0 else (), inc=("O", 16))
    E("sync", None, waits=[("O", P.cnt("O"))])


def build_nc():
    nc = bass.Bass("TRN2", target_bir_lowering=False)
    tn = {}
    tn["hsT"] = nc.dram_tensor("hsT", [D, SC], F32, kind="ExternalInput").ap()
    tn["wq"] = nc.dram_tensor("wq", [D, D], F32, kind="ExternalInput").ap()
    tn["wo"] = nc.dram_tensor("wo", [D, D], F32, kind="ExternalInput").ap()
    tn["kvc"] = nc.dram_tensor("kvc", [1024, 2, NKV, 16, HD], F32, kind="ExternalInput").ap()
    tn["bt"] = nc.dram_tensor("bt", [512], I32, kind="ExternalInput").ap()
    tn["w1qT"] = nc.dram_tensor("w1qT", [1024, HF], F32, kind="ExternalInput").ap()
    tn["w3qT"] = nc.dram_tensor("w3qT", [1024, HF], F32, kind="ExternalInput").ap()
    tn["w2qT"] = nc.dram_tensor("w2qT", [KF2, D], F32, kind="ExternalInput").ap()
    tn["a1th"] = nc.dram_tensor("a1th", [128, 64], F32, kind="ExternalInput").ap()
    tn["a3th"] = nc.dram_tensor("a3th", [128, 64], F32, kind="ExternalInput").ap()
    tn["a2th"] = nc.dram_tensor("a2th", [128, 172], F32, kind="ExternalInput").ap()
    tn["b1Th"] = nc.dram_tensor("b1Th", [2, HF], F32, kind="ExternalInput").ap()
    tn["b3Th"] = nc.dram_tensor("b3Th", [2, HF], F32, kind="ExternalInput").ap()
    tn["b2Th"] = nc.dram_tensor("b2Th", [2, D], F32, kind="ExternalInput").ap()
    tn["rc_h"] = nc.dram_tensor("rc_h", [128, SC], BF16, kind="ExternalInput").ap()
    tn["rs_h"] = nc.dram_tensor("rs_h", [128, SC], BF16, kind="ExternalInput").ap()
    tn["onesc_h"] = nc.dram_tensor("onesc_h", [128, 1], BF16, kind="ExternalInput").ap()
    tn["onesr_h"] = nc.dram_tensor("onesr_h", [1, 128], BF16, kind="ExternalInput").ap()
    tn["ident_h"] = nc.dram_tensor("ident_h", [128, 128], BF16, kind="ExternalInput").ap()
    tn["perm_h"] = nc.dram_tensor("perm_h", [128, 256], BF16, kind="ExternalInput").ap()
    tn["outT"] = nc.dram_tensor("outT", [D, SC], F32, kind="ExternalOutput").ap()

    with ExitStack() as ctx:
        sbt = lambda name, shape, dt: ctx.enter_context(nc.sbuf_tensor(name, shape, dt))
        tn["QT"] = sbt("QT", [128, 32768], BF16)
        tn["BIG"] = sbt("BIG", [128, 32768], BF16)
        tn["SLAB"] = sbt("SLAB", [128, 8192], BF16)
        tn["MISC"] = sbt("MISC", [128, 4096], BF16)
        tn["FSCR"] = sbt("FSCR", [128, 3072], F32)
        tn["cos"] = sbt("cos", [128, SC], BF16)
        tn["sin"] = sbt("sin", [128, SC], BF16)
        tn["onesc"] = sbt("onesc", [128, 1], BF16)
        tn["onesr"] = sbt("onesr", [1, 128], BF16)
        tn["ident"] = sbt("ident", [128, 128], BF16)
        tn["perm"] = sbt("perm", [128, 256], BF16)
        tn["idx_sb"] = sbt("idx_sb", [128, 4], I32)
        tn["bias_sb"] = sbt("bias_sb", [128, 8], F32)
        tn["maxq_list"] = sbt("maxq_list", [1, 64], F32)
        tn["maxk_list"] = sbt("maxk_list", [1, 16], F32)
        tn["knsum"] = sbt("knsum", [128, 64], F32)
        tn["kncol"] = sbt("kncol", [128, 1], F32)
        tn["kncolb"] = sbt("kncolb", [128, 1], BF16)
        tn["maxq"] = sbt("maxq", [1, 1], F32)
        tn["srow"] = sbt("srow", [1, 2048], F32)
        tn["statf"] = sbt("statf", [1, 2048], F32)
        tn["statb"] = sbt("statb", [1, 2112], BF16)
        tn["lrec"] = sbt("lrec", [1, 512], F32)
        tn["dena"] = sbt("dena", [128, 1024], F32)
        tn["denb"] = sbt("denb", [128, 512], BF16)
        tn["lrecb"] = sbt("lrecb", [1, 512], BF16)
        tn["a1t"] = sbt("a1t", [128, 64], BF16)
        tn["a3t"] = sbt("a3t", [128, 64], BF16)
        tn["a2t"] = sbt("a2t", [128, 172], BF16)
        tn["b1t"] = sbt("b1t", [2, 256], BF16)
        tn["b3t"] = sbt("b3t", [2, 256], BF16)
        tn["b2t"] = sbt("b2t", [2, 256], BF16)
        tn["u1b"] = sbt("u1b", [2, 1024], BF16)
        tn["u3b"] = sbt("u3b", [2, 1024], BF16)
        tn["u2b"] = sbt("u2b", [2, 1024], BF16)
        pst = lambda name, shape, dt: ctx.enter_context(nc.psum_tensor(name, shape, dt))
        tn["S0"] = pst("S0", [128, 1024], F32)
        tn["S1"] = pst("S1", [128, 1024], F32)
        tn["ACC"] = pst("ACC", [128, 512], F32)
        tn["BCAST"] = pst("BCAST", [128, 512], F32)
        tn["ROW"] = pst("ROW", [2, 512], F32)
        tn["PTB"] = pst("PTB", [128, 1024], BF16)

        semnames = ["G", "G2", "S", "O", "T", "T2", "V", "V2", "V3", "SC", "S2"]
        semh = {n: ctx.enter_context(nc.semaphore(n)) for n in semnames}

        P = Plan()
        build_plan(P, tn)

        block = ctx.enter_context(nc.Block())

        def replay(engname):
            def body(eng):
                for (e, fn, waits, incs) in P.ops:
                    if e != engname:
                        continue
                    for (s, v) in waits:
                        eng.wait_ge(semh[s], v)
                    if fn is not None:
                        inst = fn(eng)
                        for (s, amt) in incs:
                            inst = inst.then_inc(semh[s], amt)
                    else:
                        for (s, amt) in incs:
                            eng.sem_inc(semh[s], amt)
            return body

        block.sync(replay("sync"))
        block.gpsimd(replay("gpsimd"))
        block.tensor(replay("tensor"))
        block.vector(replay("vector"))
        block.scalar(replay("scalar"))

    return nc


def _host_inputs(inputs):
    hs = np.asarray(inputs["L0_attn_hidden_states"], dtype=np.float32)
    hs2 = hs.reshape(2 * 4096, D)
    hsT = np.ascontiguousarray(hs2.T)
    wq = np.ascontiguousarray(np.asarray(inputs["L0_attn_Wq"], dtype=np.float32))
    wo = np.ascontiguousarray(np.asarray(inputs["L0_attn_Wo"], dtype=np.float32))
    kvc = np.ascontiguousarray(np.asarray(inputs["L0_attn_kv_cache"], dtype=np.float32))
    bt = np.ascontiguousarray(np.asarray(inputs["L0_attn_block_table"], dtype=np.int32))
    w1qT = np.ascontiguousarray(np.asarray(inputs["L0_ffn_w1q"], dtype=np.float32).T)
    w3qT = np.ascontiguousarray(np.asarray(inputs["L0_ffn_w3q"], dtype=np.float32).T)
    w2qT = np.ascontiguousarray(np.asarray(inputs["L0_ffn_w2q"], dtype=np.float32).T)
    a1 = np.asarray(inputs["L0_ffn_a1"], dtype=np.float32)
    a3 = np.asarray(inputs["L0_ffn_a3"], dtype=np.float32)
    a2 = np.asarray(inputs["L0_ffn_a2"], dtype=np.float32)
    a1th = np.ascontiguousarray(a1.reshape(2, 32, 128).transpose(2, 1, 0).reshape(128, 64))
    a3th = np.ascontiguousarray(a3.reshape(2, 32, 128).transpose(2, 1, 0).reshape(128, 64))
    a2th = np.ascontiguousarray(a2.reshape(2, 86, 128).transpose(2, 1, 0).reshape(128, 172))
    b1Th = np.ascontiguousarray(np.asarray(inputs["L0_ffn_b1"], dtype=np.float32).T)
    b3Th = np.ascontiguousarray(np.asarray(inputs["L0_ffn_b3"], dtype=np.float32).T)
    b2Th = np.ascontiguousarray(np.asarray(inputs["L0_ffn_b2"], dtype=np.float32).T)

    onesc = np.ones((128, 1), dtype=ml_dtypes.bfloat16)
    pm = np.zeros((128, 128), dtype=np.float32)
    for i in range(64):
        pm[64 + i, i] = -1.0
        pm[i, 64 + i] = 1.0
    sh = np.zeros((128, 128), dtype=np.float32)
    for i in range(64):
        sh[64 + i, i] = 1.0
        sh[i, 64 + i] = 1.0
    perm = np.concatenate([pm, sh], axis=1).astype(ml_dtypes.bfloat16)
    onesr = np.ones((1, 128), dtype=ml_dtypes.bfloat16)
    ident = np.eye(128, dtype=ml_dtypes.bfloat16)

    invf = 10000.0 ** (-(np.arange(64, dtype=np.float64)) / 64.0)
    invf2 = np.concatenate([invf, invf])

    in_maps = []
    for core in range(NCORE):
        pos = (core % 4) * 1024 + np.arange(1024, dtype=np.float64)
        ang = invf2[:, None] * pos[None, :]
        rc = np.cos(ang).astype(ml_dtypes.bfloat16)
        rs = np.sin(ang).astype(ml_dtypes.bfloat16)
        m = {
            "hsT": np.ascontiguousarray(hsT[:, core * SC:(core + 1) * SC]),
            "wq": wq, "wo": wo, "kvc": kvc, "bt": bt,
            "w1qT": w1qT, "w3qT": w3qT, "w2qT": w2qT,
            "a1th": a1th, "a3th": a3th, "a2th": a2th,
            "b1Th": b1Th, "b3Th": b3Th, "b2Th": b2Th,
            "rc_h": rc, "rs_h": rs,
            "onesc_h": onesc, "onesr_h": onesr, "ident_h": ident, "perm_h": perm,
        }
        in_maps.append(m)
    return in_maps


def kernel(**inputs):
    in_maps = _host_inputs(inputs)
    nc = build_nc()
    res = run_bass_kernel_spmd(nc, in_maps, core_ids=list(range(NCORE)))
    outT = np.concatenate([r["outT"] for r in res.results], axis=1)
    return np.ascontiguousarray(outT.T).reshape(2, 4096, D).astype(np.float32)


# version 12
# speedup vs baseline: 1.0934x; 1.0092x over previous
import numpy as np
import ml_dtypes
import concourse.bass as bass
import concourse.mybir as mybir
from concourse.bass_utils import run_bass_kernel_spmd
from contextlib import ExitStack

F32 = mybir.dt.float32
BF16 = mybir.dt.bfloat16
I32 = mybir.dt.int32
AF = mybir.ActivationFunctionType
AX = mybir.AxisListType

D = 4096
SC = 1024
NKV = 8
HD = 128
HF = 11008
KF2 = 2752
NCORE = 8
INVSQ = 0.08838834764831843
LORA_SCALE = 0.5

STAGE = 5


class Plan:
    def __init__(self):
        self.ops = []
        self.sem = {}

    def cnt(self, name):
        return self.sem.get(name, 0)

    def emit(self, eng, fn=None, waits=(), inc=None):
        ws = [(s, v) for (s, v) in waits if v > 0]
        incs = []
        if inc is not None:
            for (s, amt) in ([inc] if isinstance(inc, tuple) else inc):
                self.sem[s] = self.sem.get(s, 0) + amt
                incs.append((s, amt))
        self.ops.append((eng, fn, ws, incs))


def build_plan(P, tn):
    E = P.emit
    C = P.cnt
    QT = tn["QT"]; BIG = tn["BIG"]; SLAB = tn["SLAB"]; MISC = tn["MISC"]
    FSCR = tn["FSCR"]
    S0 = tn["S0"]; S1 = tn["S1"]; ACC = tn["ACC"]; BCAST = tn["BCAST"]
    ROW = tn["ROW"]; PTB = tn["PTB"]
    cos = tn["cos"]; sin = tn["sin"]; onesc = tn["onesc"]; onesr = tn["onesr"]
    ident = tn["ident"]; perm = tn["perm"]
    idx_sb = tn["idx_sb"]; bias_sb = tn["bias_sb"]
    maxq_list = tn["maxq_list"]; maxk_list = tn["maxk_list"]
    knsum = tn["knsum"]; kncol = tn["kncol"]; kncolb = tn["kncolb"]
    maxq = tn["maxq"]; srow = tn["srow"]; statf = tn["statf"]; statb = tn["statb"]
    lrec = tn["lrec"]; lrecb = tn["lrecb"]; dena = tn["dena"]; denb = tn["denb"]
    a1t = tn["a1t"]; a3t = tn["a3t"]; a2t = tn["a2t"]
    b1t = tn["b1t"]; b3t = tn["b3t"]; b2t = tn["b2t"]
    u1b = tn["u1b"]; u3b = tn["u3b"]; u2b = tn["u2b"]
    hsT = tn["hsT"]; wq = tn["wq"]; wo = tn["wo"]; kvc = tn["kvc"]; bt = tn["bt"]
    w1qT = tn["w1qT"]; w3qT = tn["w3qT"]; w2qT = tn["w2qT"]
    outT = tn["outT"]

    E("gpsimd", lambda e: e.dma_start(out=idx_sb[:, :], in_=bt.rearrange("(g p) -> p g", p=128),
                                      allow_slow_non_contiguous=True), inc=("G", 16))
    E("gpsimd", lambda e: e.dma_start(out=a1t[:, :], in_=tn["a1th"][:, :]), inc=("G", 16))
    E("gpsimd", lambda e: e.dma_start(out=a3t[:, :], in_=tn["a3th"][:, :]), inc=("G", 16))
    E("gpsimd", lambda e: e.dma_start(out=a2t[:, :], in_=tn["a2th"][:, :]), inc=("G", 16))
    g_consts = C("G")
    E("sync", lambda e: e.dma_start(out=cos[:, :], in_=tn["rc_h"][:, :]), inc=("S", 16))
    E("sync", lambda e: e.dma_start(out=sin[:, :], in_=tn["rs_h"][:, :]), inc=("S", 16))
    E("sync", lambda e: e.dma_start(out=onesc[:, :], in_=tn["onesc_h"][:, :]), inc=("S", 16))
    E("sync", lambda e: e.dma_start(out=onesr[:, :], in_=tn["onesr_h"][:, :]), inc=("S", 16))
    E("sync", lambda e: e.dma_start(out=ident[:, :], in_=tn["ident_h"][:, :]), inc=("S", 16))
    E("sync", lambda e: e.dma_start(out=perm[:, :], in_=tn["perm_h"][:, :]), inc=("S", 16))
    s_consts = C("S")

    for c in range(32):
        E("gpsimd", lambda e, c=c: e.dma_start(out=BIG[:, c * 1024:(c + 1) * 1024],
                                               in_=hsT[c * 128:(c + 1) * 128, :]), inc=("G", 16))
    g_hs = C("G")

    wq3 = wq.rearrange("(c p) q -> p c q", p=128)
    t_slabfree = {}
    v_qdone = {}
    v_qk = {}
    sc_sq = {}
    qn_idx = 0
    for dq in range(32):
        buf = dq % 2
        waits = [("T", t_slabfree[dq - 2])] if dq >= 2 else []
        E("gpsimd", lambda e, dq=dq, buf=buf: e.dma_start(
            out=SLAB[:, buf * 4096:(buf + 1) * 4096].rearrange("p (c q) -> p c q", q=128),
            in_=wq3[:, :, dq * 128:(dq + 1) * 128]), waits=waits, inc=("G", 16))
        g_slab = C("G")
        ps = [S0, S1][buf]
        for st in range(2):
            for dk in range(32):
                waits = []
                if dk == 0:
                    waits.append(("G", g_slab))
                    if dq == 0 and st == 0:
                        waits.append(("G", g_hs))
                    if dq >= 2:
                        waits.append(("V", v_qdone[(dq - 2, st)]))
                        waits.append(("SC", sc_sq[(dq - 2, st)]))
                E("tensor", lambda e, ps=ps, st=st, buf=buf, dk=dk: e.matmul(
                    ps[:, st * 512:(st + 1) * 512],
                    SLAB[:, buf * 4096 + dk * 128: buf * 4096 + (dk + 1) * 128],
                    BIG[:, dk * 1024 + st * 512: dk * 1024 + st * 512 + 512],
                    start=(dk == 0), stop=(dk == 31), skip_group_check=True),
                    waits=waits,
                    inc=("T", 1) if dk == 31 else None)
            t_q = C("T")
            if st == 1:
                t_slabfree[dq] = t_q
            qsl = ps[:, st * 512:(st + 1) * 512]
            cosl = cos[:, st * 512:(st + 1) * 512]
            sinl = sin[:, st * 512:(st + 1) * 512]
            qa = FSCR[:, 0:512]
            qb = FSCR[:, 512:1024]
            base = dq * 1024 + st * 512
            k = dq * 2 + st
            qsb = MISC[:, 3072 + (k % 2) * 512: 3072 + (k % 2) * 512 + 512]
            ph = [ACC, BCAST][k % 2][:, :]
            waits = [("T", t_q)]
            if k >= 2:
                waits.append(("V", v_qk[k - 2]))
            E("scalar", lambda e, qsb=qsb, qsl=qsl: e.copy(qsb, qsl), waits=waits,
              inc=("SC", 1))
            sc_ev = C("SC")
            E("scalar", lambda e, qsl=qsl: e.activation(MISC[:, 2560:3072], qsl, AF.Square),
              inc=("SC", 1))
            sc_sq[(dq, st)] = C("SC")
            waits = [("SC", sc_ev), ("S", s_consts)]
            if k >= 2:
                waits.append(("V", v_qk[k - 2]))
            E("tensor", lambda e, ph=ph, qsb=qsb: e.matmul(
                ph, perm[:, 0:128], qsb, start=True, stop=True, skip_group_check=True),
                waits=waits, inc=("T2", 1))
            t_rot = C("T2")
            E("vector", lambda e, qsb=qsb, cosl=cosl, qa=qa: e.tensor_mul(qa, qsb, cosl),
              waits=[("SC", sc_ev)])
            E("vector", lambda e, ph=ph, sinl=sinl, qb=qb: e.tensor_mul(qb, ph, sinl),
              waits=[("T2", t_rot)])
            E("vector", lambda e, qa=qa, qb=qb, base=base: e.tensor_add(
                QT[:, base:base + 512], qa, qb), inc=("V", 1))
            v_qdone[(dq, st)] = C("V")
            v_qk[k] = C("V")
            waits = [("V", v_qdone[(dq, st)]), ("SC", sc_sq[(dq, st)])]
            if qn_idx >= 1:
                waits.append(("V2", qn_idx))
            E("tensor", lambda e: e.matmul(ROW[0:1, 0:512], onesc[:, :], MISC[:, 2560:3072],
                                           start=True, stop=True, skip_group_check=True),
              waits=waits, inc=("T2", 1))
            t_n = C("T2")
            E("vector", lambda e, i=qn_idx: e.reduce_max(maxq_list[0:1, i:i + 1],
                                                         ROW[0:1, 0:512], axis=AX.X),
              waits=[("T2", t_n)], inc=("V2", 1))
            qn_idx += 1
    E("vector", lambda e: e.reduce_max(maxq[0:1, 0:1], maxq_list[0:1, 0:64], axis=AX.X),
      waits=[("V2", qn_idx)], inc=("V", 1))
    v_maxq = C("V")
    v_qt_done = C("V")
    t_phaseA = C("T")

    if STAGE == 1:
        dump_and_finish(P, tn, QT, [("V", v_qt_done)])
        return

    g_gather = {}
    t_attn_end = {}
    v_ktdone = {}
    s_bias = {}

    def emit_gather(h, extra_waits=()):
        par = h % 2
        waits = [("G", g_consts)] + list(extra_waits)
        if h >= 2:
            waits.append(("T", t_attn_end[h - 2]))
        kvflat = kvc.rearrange("b c hh s d -> b (c hh s d)")
        for cc in range(2):
            eoff = cc * 16384 + h * 2048
            for g in range(4):
                dst = BIG[:, par * 16384 + cc * 8192 + g * 2048:
                          par * 16384 + cc * 8192 + (g + 1) * 2048]
                E("gpsimd", lambda e, dst=dst, g=g, eoff=eoff: e.indirect_dma_start(
                    dst, None, kvflat,
                    bass.IndirectOffsetOnAxis(ap=idx_sb[:, g:g + 1], axis=0),
                    element_offset=eoff),
                    waits=waits if (cc == 0 and g == 0) else (), inc=("G", 16))
        g_gather[h] = C("G")

    def emit_head_prep(h):
        par = h % 2
        kbase = par * 16384
        for q8 in range(8):
            for c8 in range(8):
                cc = q8 * 8 + c8
                waits = []
                if q8 == 0 and c8 == 0:
                    waits.append(("G", g_gather[h]))
                if c8 == 0 and q8 >= 1:
                    waits.append(("V3", C("V3")))
                E("tensor", lambda e, kbase=kbase, cc=cc, c8=c8: e.transpose(
                    PTB[:, c8 * 128:(c8 + 1) * 128],
                    BIG[:, kbase + cc * 128:kbase + (cc + 1) * 128], ident[:, :]),
                    waits=waits, inc=("T", 1) if c8 == 7 else None)
            t_tr = C("T")
            E("vector", lambda e, kbase=kbase, q8=q8: e.tensor_copy(
                BIG[:, kbase + q8 * 1024:kbase + (q8 + 1) * 1024], PTB[:, :]),
                waits=[("T", t_tr)], inc=("V3", 1))
        v_ktdone[h] = C("V3")
        sqs = MISC[:, 2048:4096]
        for g in range(4):
            kgs = BIG[:, kbase + g * 2048: kbase + (g + 1) * 2048]
            waits = [("G", g_gather[h])]
            if h < 2 and g == 0:
                waits += [("T2", C("T2")), ("V2", C("V2"))]
            E("vector", lambda e, sqs=sqs, kgs=kgs: e.tensor_mul(sqs, kgs, kgs), waits=waits)
            E("vector", lambda e, sqs=sqs, g=g: e.tensor_reduce(
                knsum[:, g * 16:g * 16 + 16],
                sqs.rearrange("p (s d) -> p s d", d=128), axis=AX.X,
                op=mybir.AluOpType.add), inc=("V2", 1))
        E("vector", lambda e: e.reduce_max(kncol[:, 0:1], knsum[:, 0:64], axis=AX.X))
        E("vector", lambda e: e.tensor_copy(kncolb[:, 0:1], kncol[:, 0:1]), inc=("V2", 1))
        v_kn = C("V2")
        w2_ = [("V2", v_kn), ("V", C("V"))]
        E("tensor", lambda e: e.matmul(ROW[0:1, 0:128], kncolb[:, 0:1], ident[:, :],
                                       start=True, stop=True, skip_group_check=True),
          waits=w2_, inc=("T2", 1))
        t_kt = C("T2")
        E("vector", lambda e: e.reduce_max(srow[0:1, 0:1], ROW[0:1, 0:128], axis=AX.X),
          waits=[("T2", t_kt), ("V", v_maxq)])
        E("vector", lambda e: e.tensor_add(srow[0:1, 1:2], srow[0:1, 0:1], maxq[0:1, 0:1]))
        E("vector", lambda e: e.tensor_scalar_mul(srow[0:1, 2:3], srow[0:1, 1:2],
                                                  -0.5 * INVSQ))
        E("vector", lambda e: e.tensor_copy(statb[0:1, 0:1], srow[0:1, 2:3]), inc=("V2", 1))
        v_bb = C("V2")
        E("tensor", lambda e: e.matmul(BCAST[:, 0:1], onesr[:, :], statb[0:1, 0:1],
                                       start=True, stop=True, skip_group_check=True),
          waits=[("V2", v_bb), ("V", C("V"))], inc=("T2", 1))
        E("scalar", lambda e, h=h: e.copy(bias_sb[:, h:h + 1], BCAST[:, 0:1]),
          waits=[("T2", C("T2"))], inc=("S2", 1))
        s_bias[h] = C("S2")

    QT3 = QT.ap().rearrange("p (hh s) -> p hh s", s=1024)
    exp_tgt = {}
    pair_tgt = {}
    den_tgt = {}
    den_v = {}
    t_den_prev = [0]
    v_lrec_prev = [0]
    Kglob = 0

    emit_gather(0, [("T", t_phaseA)])
    emit_gather(1, [("T", t_phaseA)])
    emit_head_prep(0)
    emit_head_prep(1)
    if STAGE == 20:
        dump_and_finish(P, tn, BIG, [("V3", v_ktdone[1]), ("S2", s_bias[1])])
        return

    for h in range(NKV):
        par = h % 2
        kbase = par * 16384
        vbase = par * 16384 + 8192
        for st in range(8):
            qcols = QT3[:, 4 * h:4 * h + 4, st * 128:(st + 1) * 128]
            def emit_scores(pr):
                K = Kglob0 + pr
                cA, cB = 2 * pr, 2 * pr + 1
                sb = [S0, S1][pr % 2]
                waits = []
                if (K - 2) in exp_tgt:
                    waits.append(("SC", exp_tgt[K - 2]))
                if pr == 0:
                    waits += [("V3", v_ktdone[h]), ("S2", s_bias[h])]
                E("tensor", lambda e, sb=sb, kbase=kbase, cA=cA, qcols=qcols: e.matmul(
                    sb[:, 0:512], BIG[:, kbase + cA * 128:kbase + (cA + 1) * 128], qcols,
                    start=True, stop=True, skip_group_check=True), waits=waits)
                E("tensor", lambda e, sb=sb, kbase=kbase, cB=cB, qcols=qcols: e.matmul(
                    sb[:, 512:1024], BIG[:, kbase + cB * 128:kbase + (cB + 1) * 128], qcols,
                    start=True, stop=True, skip_group_check=True), inc=("T", 1))
                t_sc = C("T")
                waits = [("T", t_sc)]
                if (K - 2) in pair_tgt:
                    waits.append(("T", pair_tgt[K - 2]))
                    if den_v.get(K - 2) is not None:
                        waits.append(("V", den_v[K - 2]))
                ptv = MISC[:, (pr % 2) * 1024:(pr % 2 + 1) * 1024]
                E("scalar", lambda e, ptv=ptv, sb=sb, h=h: e.activation(
                    ptv, sb[:, :], AF.Exp, bias=bias_sb[:, h:h + 1], scale=INVSQ),
                    waits=waits, inc=("SC", 1))
                exp_tgt[K] = C("SC")

            def emit_ctxden(pr):
                K = Kglob0 + pr
                cA, cB = 2 * pr, 2 * pr + 1
                ptv = MISC[:, (pr % 2) * 1024:(pr % 2 + 1) * 1024]
                waits = [("SC", exp_tgt[K])]
                if pr == 0 and (h, st) != (0, 0):
                    waits.append(("S2", C("S2")))
                    waits.append(("V", C("V")))
                if pr == 0 and (h, st) == (0, 0):
                    waits.append(("V", v_qt_done))
                E("tensor", lambda e, vbase=vbase, cA=cA, ptv=ptv, pr=pr: e.matmul(
                    ACC[:, :], BIG[:, vbase + cA * 128:vbase + (cA + 1) * 128], ptv[:, 0:512],
                    start=(pr == 0), stop=False, skip_group_check=True), waits=waits)
                E("tensor", lambda e, vbase=vbase, cB=cB, ptv=ptv, pr=pr: e.matmul(
                    ACC[:, :], BIG[:, vbase + cB * 128:vbase + (cB + 1) * 128], ptv[:, 512:1024],
                    start=False, stop=(pr == 31), skip_group_check=True), inc=("T", 1))
                if False:
                    w2_ = []
                    if pr == 0:
                        w2_.append(("V2", C("V2")))
                    E("tensor", lambda e, ptv=ptv, pr=pr: e.matmul(
                        ROW[0:1, 0:512], onesc[:, :], ptv[:, 0:512],
                        start=(pr == 0), stop=False, skip_group_check=True), waits=w2_)
                    E("tensor", lambda e, ptv=ptv: e.matmul(
                        ROW[0:1, 0:512], onesc[:, :], ptv[:, 512:1024],
                        start=False, stop=False, skip_group_check=True), inc=("T", 1))
                    pair_tgt[K] = C("T")
                    den_v[K] = None
                else:
                    pair_tgt[K] = C("T")
                    dw = [("SC", exp_tgt[K])]
                    if pr == 0:
                        E("vector", lambda e, ptv=ptv: e.tensor_copy(dena[:, :], ptv[:, :]),
                          waits=dw, inc=("V", 1))
                    else:
                        E("vector", lambda e, ptv=ptv: e.tensor_add(dena[:, :], dena[:, :],
                                                                    ptv[:, :]), waits=dw,
                          inc=("V", 1))
                    den_v[K] = C("V")

            Kglob0 = Kglob
            for pr in range(32):
                emit_scores(pr)
                if pr >= 1:
                    emit_ctxden(pr - 1)
            emit_ctxden(31)
            Kglob += 32
            t_group = C("T")
            E("vector", lambda e: e.tensor_add(dena[:, 0:512], dena[:, 0:512],
                                               dena[:, 512:1024]))
            dbw = [("T2", t_den_prev[0])] if t_den_prev[0] else []
            E("vector", lambda e: e.tensor_copy(denb[:, :], dena[:, 0:512]), waits=dbw,
              inc=("V", 1))
            v_db = C("V")
            E("tensor", lambda e: e.matmul(ROW[0:1, 0:512], onesc[:, :], denb[:, :],
                                           start=True, stop=True, skip_group_check=True),
              waits=[("V", v_db), ("V2", C("V2"))], inc=("T2", 1))
            t_den = C("T2")
            t_den_prev[0] = t_den
            E("vector", lambda e: e.reciprocal(lrec[0:1, 0:512], ROW[0:1, 0:512]),
              waits=[("T2", t_den)])
            E("vector", lambda e: e.tensor_copy(lrecb[0:1, 0:512], lrec[0:1, 0:512]),
              inc=("V", 1))
            v_l = C("V")
            E("tensor", lambda e: e.matmul(BCAST[:, 0:512], onesr[:, :], lrecb[0:1, 0:512],
                                           start=True, stop=True, skip_group_check=True),
              waits=[("V", v_l)], inc=("T2", 1))
            t_bc = C("T2")
            E("scalar", lambda e: e.copy(FSCR[:, 1024:1536], ACC[:, :]),
              waits=[("T", t_group)], inc=("S2", 1))
            s_ce = C("S2")
            ctxe3 = FSCR[:, 1024:1536].rearrange("p (g s) -> p g s", s=128)
            bc3 = BCAST[:, 0:512].rearrange("p (g s) -> p g s", s=128)
            E("vector", lambda e, qcols=qcols, ctxe3=ctxe3, bc3=bc3: e.tensor_mul(
                qcols, ctxe3, bc3),
              waits=[("T2", t_bc), ("S2", s_ce)], inc=("V", 1))
        t_attn_end[h] = C("T")
        if h + 2 < NKV:
            emit_gather(h + 2)
            emit_head_prep(h + 2)
    v_attn_done = C("V")
    sc_attn_done = C("SC")

    if STAGE == 2:
        dump_and_finish(P, tn, QT, [("V", v_attn_done)])
        return

    wo3 = wo.rearrange("(c p) q -> p c q", p=128)
    t_woslabfree = {}
    s_woevac = {}
    widx = 0
    for do in range(32):
        buf = do % 2
        waits = [("T", t_woslabfree[do - 2])] if do >= 2 else []
        E("gpsimd", lambda e, do=do, buf=buf: e.dma_start(
            out=SLAB[:, buf * 4096:(buf + 1) * 4096].rearrange("p (c q) -> p c q", q=128),
            in_=wo3[:, :, do * 128:(do + 1) * 128]), waits=waits, inc=("G", 16))
        g_woslab = C("G")
        for st2 in range(2):
            bank = [ACC, BCAST][widx % 2]
            for dk in range(32):
                waits = []
                if dk == 0:
                    waits.append(("G", g_woslab))
                    if widx == 0:
                        waits.append(("V", v_attn_done))
                    if widx >= 2:
                        waits.append(("S2", s_woevac[widx - 2]))
                E("tensor", lambda e, bank=bank, buf=buf, dk=dk, st2=st2: e.matmul(
                    bank[:, :], SLAB[:, buf * 4096 + dk * 128:buf * 4096 + (dk + 1) * 128],
                    QT[:, dk * 1024 + st2 * 512: dk * 1024 + st2 * 512 + 512],
                    start=(dk == 0), stop=(dk == 31), skip_group_check=True),
                    waits=waits, inc=("T", 1) if dk == 31 else None)
            t_w = C("T")
            if st2 == 1:
                t_woslabfree[do] = t_w
            E("scalar", lambda e, bank=bank, do=do, st2=st2: e.copy(
                BIG[:, do * 1024 + st2 * 512: do * 1024 + st2 * 512 + 512], bank[:, :]),
                waits=[("T", t_w)], inc=("S2", 1))
            s_woevac[widx] = C("S2")
            widx += 1
    s_wo_done = C("S2")
    if STAGE == 30:
        dump_and_finish(P, tn, BIG, [("S2", s_wo_done)])
        return
    t_stats = {}
    t_statmm = {}
    t_bcast = {}
    v_x1 = {}
    rb_banks = {0: ACC[:, :], 1: S0[:, 0:512]}
    mb_banks = {0: BCAST[:, :], 1: S0[:, 512:1024]}

    def emit_pass1(st2):
        ps = [S0, S1][st2]
        for c in range(32):
            slot = FSCR[:, 2048 + (c % 2) * 512:2048 + (c % 2) * 512 + 512]
            waits = [] if (st2 == 0 and c < 2) else [("V", C("V"))]
            E("sync", lambda e, c=c, st2=st2, slot=slot: e.dma_start(
                out=slot, in_=hsT[c * 128:(c + 1) * 128, st2 * 512:(st2 + 1) * 512]),
                waits=waits, inc=("S", 16))
            s_x0 = C("S")
            k6 = st2 * 32 + c
            sq = MISC[:, 2560 + (k6 % 2) * 512: 2560 + (k6 % 2) * 512 + 512]
            xcols = BIG[:, c * 1024 + st2 * 512: c * 1024 + st2 * 512 + 512]
            E("vector", lambda e, xcols=xcols, slot=slot: e.tensor_add(
                xcols, slot, xcols),
                waits=[("S", s_x0), ("S2", s_woevac[c * 2 + st2])])
            waits = [] if k6 < 2 else [("T", t_statmm[k6 - 2])]
            E("vector", lambda e, xcols=xcols, sq=sq: e.tensor_mul(sq, xcols, xcols),
              waits=waits, inc=("V", 1))
            v_x = C("V")
            waits = [("V", v_x)]
            if st2 == 0 and c == 0:
                waits.append(("SC", sc_attn_done))
            E("tensor", lambda e, ps=ps, xcols=xcols, c=c: e.matmul(
                ps[0:1, 0:512], onesc[:, :], xcols,
                start=(c == 0), stop=(c == 31), skip_group_check=True), waits=waits)
            E("tensor", lambda e, ps=ps, c=c, sq=sq: e.matmul(
                ps[0:1, 512:1024], onesc[:, :], sq,
                start=(c == 0), stop=(c == 31), skip_group_check=True), inc=("T", 1))
            t_statmm[k6] = C("T")
        t_stats[st2] = C("T")

    def emit_finalize(st2, t_wait, rb, mb, statoff):
        ps = [S0, S1][st2]
        E("vector", lambda e, ps=ps, st2=st2, statoff=statoff: e.tensor_copy(
            srow[0:1, statoff + st2 * 1024: statoff + st2 * 1024 + 1024], ps[0:1, 0:1024]),
            waits=[("T", t_wait)], inc=("V", 1))
        v_se = C("V")
        o = statoff + st2 * 1024
        E("vector", lambda e, o=o: e.tensor_scalar_mul(
            statf[0:1, 0:512], srow[0:1, o:o + 512], 1.0 / 4096.0), waits=[("V", v_se)])
        E("vector", lambda e, o=o: e.tensor_scalar_mul(
            statf[0:1, 512:1024], srow[0:1, o + 512:o + 1024], 1.0 / 4096.0))
        E("vector", lambda e: e.tensor_mul(statf[0:1, 1024:1536],
                                           statf[0:1, 0:512], statf[0:1, 0:512]))
        E("vector", lambda e: e.tensor_sub(statf[0:1, 512:1024],
                                           statf[0:1, 512:1024], statf[0:1, 1024:1536]),
          inc=("V", 1))
        v_var = C("V")
        E("scalar", lambda e: e.activation(statf[0:1, 1536:2048], statf[0:1, 512:1024],
                                           AF.Ln, bias=np.float32(1e-5)), waits=[("V", v_var)])
        E("scalar", lambda e: e.activation(statf[0:1, 1024:1536], statf[0:1, 1536:2048],
                                           AF.Exp, scale=-0.5), inc=("S2", 1))
        s_rs = C("S2")
        E("vector", lambda e: e.tensor_mul(statf[0:1, 1536:2048], statf[0:1, 0:512],
                                           statf[0:1, 1024:1536]), waits=[("S2", s_rs)])
        E("vector", lambda e, st2=st2: e.tensor_copy(
            statb[0:1, 64 + st2 * 1024: 64 + st2 * 1024 + 512], statf[0:1, 1024:1536]))
        E("vector", lambda e, st2=st2: e.tensor_copy(
            statb[0:1, 64 + st2 * 1024 + 512: 64 + st2 * 1024 + 1024],
            statf[0:1, 1536:2048]), inc=("V", 1))
        v_b = C("V")
        E("tensor", lambda e, st2=st2, rb=rb: e.matmul(
            rb, onesr[:, :], statb[0:1, 64 + st2 * 1024: 64 + st2 * 1024 + 512],
            start=True, stop=True, skip_group_check=True), waits=[("V", v_b), ("V", C("V"))])
        E("tensor", lambda e, st2=st2, mb=mb: e.matmul(
            mb, onesr[:, :], statb[0:1, 64 + st2 * 1024 + 512: 64 + st2 * 1024 + 1024],
            start=True, stop=True, skip_group_check=True), inc=("T2", 1))
        return C("T2")

    def emit_pass2(st2):
        for c in range(32):
            xbc = BIG[:, c * 1024 + st2 * 512: c * 1024 + st2 * 512 + 512]
            xcols = QT[:, c * 1024 + st2 * 512: c * 1024 + st2 * 512 + 512]
            t0 = FSCR[:, 1536:2048]
            E("vector", lambda e, xbc=xbc, rb=rb_banks[st2], t0=t0: e.tensor_mul(
                t0, xbc, rb), waits=[("T2", t_bcast[st2])])
            E("vector", lambda e, xcols=xcols, mb=mb_banks[st2], t0=t0: e.tensor_sub(
                xcols, t0, mb), inc=("V", 1))
            v_x1[(st2, c)] = C("V")

    def emit_lora_u(st2):
        for c in range(32):
            xcols = QT[:, c * 1024 + st2 * 512: c * 1024 + st2 * 512 + 512]
            waits = [("V", v_x1[(st2, c)])]
            if c == 0:
                waits.append(("V", C("V")))
            E("tensor", lambda e, c=c, xcols=xcols: e.matmul(
                ROW[0:2, 0:512], a1t[:, c * 2:c * 2 + 2], xcols,
                start=(c == 0), stop=(c == 31), skip_group_check=True), waits=waits)
            E("tensor", lambda e, c=c, xcols=xcols: e.matmul(
                S1[0:2, 0:512], a3t[:, c * 2:c * 2 + 2], xcols,
                start=(c == 0), stop=(c == 31), skip_group_check=True),
                inc=("T", 1) if c == 31 else None)
        t_u = C("T")
        E("vector", lambda e, st2=st2: e.tensor_scalar_mul(
            u1b[0:2, st2 * 512:(st2 + 1) * 512], ROW[0:2, 0:512], LORA_SCALE),
            waits=[("T", t_u)])
        E("vector", lambda e, st2=st2: e.tensor_scalar_mul(
            u3b[0:2, st2 * 512:(st2 + 1) * 512], S1[0:2, 0:512], LORA_SCALE), inc=("V", 1))

    emit_pass1(0)
    t_bcast[0] = emit_finalize(0, t_stats[0], rb_banks[0], mb_banks[0], 0)
    if STAGE == 31:
        dump_and_finish(P, tn, BIG, [("T2", t_bcast[0])])
        return
    emit_pass1(1)
    emit_pass2(0)
    if STAGE == 32:
        dump_and_finish(P, tn, QT, [("V", C("V")), ("T", t_stats[1])])
        return
    t_bcast[1] = emit_finalize(1, t_stats[1], rb_banks[1], mb_banks[1], 0)
    emit_pass2(1)
    if STAGE == 33:
        dump_and_finish(P, tn, QT, [("V", C("V"))])
        return
    emit_lora_u(0)
    emit_lora_u(1)
    v_ln1_done = C("V")

    for st2 in range(2):
        for j in range(8):
            sl = [QT[:, cc * 1024 + st2 * 512: cc * 1024 + st2 * 512 + 512]
                  for cc in (j, j + 8, j + 16, j + 24)]
            dst = BIG[:, 22528 + j * 1024 + st2 * 512: 22528 + j * 1024 + st2 * 512 + 512]
            E("vector", lambda e, sl=sl: e.tensor_add(FSCR[:, 2048:2560], sl[0], sl[1]),
              waits=[("V", v_ln1_done)])
            E("vector", lambda e, sl=sl: e.tensor_add(FSCR[:, 2560:3072], sl[2], sl[3]))
            E("vector", lambda e, dst=dst: e.tensor_add(
                dst, FSCR[:, 2048:2560], FSCR[:, 2560:3072]), inc=("V", 1))
    v_xf_done = C("V")

    if STAGE == 3:
        dump_and_finish(P, tn, QT, [("V", v_xf_done)])
        return

    w1q3 = w1qT.rearrange("(kc p) o -> p kc o", p=128)
    w3q3 = w3qT.rearrange("(kc p) o -> p kc o", p=128)
    oc_list = []
    for i in range(43):
        oc_list += [i, i + 43]
    t_fslabfree = {}
    v_y = {}
    u2_tgt = {}
    pend_u2 = []
    t_zrot = {0: 0, 1: 0}
    for n, oc in enumerate(oc_list):
        buf = n % 2
        i = n // 2
        sel = n - 2 * i
        waits = [("V", v_xf_done)] if n < 2 else [("T", t_fslabfree[oc_list[n - 2]])]
        E("gpsimd", lambda e, oc=oc, buf=buf: e.dma_start(
            out=SLAB[:, buf * 2048:buf * 2048 + 1024].rearrange("p (kc o) -> p kc o", o=128),
            in_=w1q3[:, :, oc * 128:(oc + 1) * 128]), waits=waits, inc=("G", 16))
        E("gpsimd", lambda e, oc=oc, buf=buf: e.dma_start(
            out=SLAB[:, buf * 2048 + 1024:buf * 2048 + 2048].rearrange("p (kc o) -> p kc o", o=128),
            in_=w3q3[:, :, oc * 128:(oc + 1) * 128]), inc=("G", 16))
        E("gpsimd", lambda e, oc=oc, buf=buf: e.dma_start(
            out=b1t[0:2, buf * 128:(buf + 1) * 128], in_=tn["b1Th"][:, oc * 128:(oc + 1) * 128]),
            inc=("G", 16))
        E("gpsimd", lambda e, oc=oc, buf=buf: e.dma_start(
            out=b3t[0:2, buf * 128:(buf + 1) * 128], in_=tn["b3Th"][:, oc * 128:(oc + 1) * 128]),
            inc=("G", 16))
        g_fslab = C("G")
        for (poc, pst2, pysl, pn) in pend_u2:
            u2bank = [ROW[0:2, 0:512], BCAST[0:2, 0:512]][pst2]
            E("tensor", lambda e, poc=poc, pysl=pysl, u2bank=u2bank, pn=pn: e.matmul(
                u2bank, a2t[:, poc * 2:poc * 2 + 2], pysl,
                start=(pn == 0), stop=(pn == len(oc_list) - 1), skip_group_check=True),
                waits=[("V", v_y[(poc, pst2)])], inc=("T2", 1))
            u2_tgt[(poc, pst2)] = C("T2")
        pend_u2 = []
        for st2 in range(2):
            gb = [S0[:, 0:512], S1[:, 0:512]][n % 2]
            ub = [S0[:, 512:1024], S1[:, 512:1024]][n % 2]
            for k in range(8):
                waits = []
                if k == 0:
                    waits.append(("G", g_fslab))
                    if n >= 2 and st2 == 0:
                        waits.append(("V", v_y[(oc_list[n - 2], 0)]))
                        waits.append(("V", v_y[(oc_list[n - 2], 1)]))
                E("tensor", lambda e, gb=gb, buf=buf, k=k, st2=st2: e.matmul(
                    gb, SLAB[:, buf * 2048 + k * 128:buf * 2048 + (k + 1) * 128],
                    BIG[:, 22528 + k * 1024 + st2 * 512: 22528 + k * 1024 + st2 * 512 + 512],
                    start=(k == 0), stop=False, skip_group_check=True), waits=waits)
            E("tensor", lambda e, gb=gb, buf=buf, st2=st2: e.matmul(
                gb, b1t[0:2, buf * 128:(buf + 1) * 128], u1b[0:2, st2 * 512:(st2 + 1) * 512],
                start=False, stop=True, skip_group_check=True), inc=("T", 1))
            t_g = C("T")
            for k in range(8):
                E("tensor", lambda e, ub=ub, buf=buf, k=k, st2=st2: e.matmul(
                    ub, SLAB[:, buf * 2048 + 1024 + k * 128:buf * 2048 + 1024 + (k + 1) * 128],
                    BIG[:, 22528 + k * 1024 + st2 * 512: 22528 + k * 1024 + st2 * 512 + 512],
                    start=(k == 0), stop=False, skip_group_check=True))
            E("tensor", lambda e, ub=ub, buf=buf, st2=st2: e.matmul(
                ub, b3t[0:2, buf * 128:(buf + 1) * 128], u3b[0:2, st2 * 512:(st2 + 1) * 512],
                start=False, stop=True, skip_group_check=True), inc=("T", 1))
            t_up = C("T")
            if st2 == 1:
                t_fslabfree[oc] = t_up
            E("scalar", lambda e, gb=gb: e.activation(MISC[:, 2048:2560], gb, AF.Silu),
              waits=[("T", t_g)], inc=("SC", 1))
            s_sil = C("SC")
            ysl = BIG[:, 30720 + sel * 1024 + st2 * 512: 30720 + sel * 1024 + st2 * 512 + 512]
            yw = [("SC", s_sil), ("T", t_up)]
            if n >= 2 and (oc_list[n - 2], st2) in u2_tgt:
                yw.append(("T2", u2_tgt[(oc_list[n - 2], st2)]))
            E("vector", lambda e, ub=ub, ysl=ysl: e.tensor_mul(ysl, MISC[:, 2048:2560], ub),
              waits=yw, inc=("V", 1))
            v_y[(oc, st2)] = C("V")
            pend_u2.append((oc, st2, ysl, n))
        if sel == 1:
            for st2 in range(2):
                yA = BIG[:, 30720 + st2 * 512: 30720 + st2 * 512 + 512]
                yB = BIG[:, 31744 + st2 * 512: 31744 + st2 * 512 + 512]
                w = [("V", v_y[(i, st2)]), ("V", v_y[(i + 43, st2)])]
                if i <= 20:
                    yfj = BIG[:, i * 1024 + st2 * 512: i * 1024 + st2 * 512 + 512]
                    E("vector", lambda e, yfj=yfj, yA=yA, yB=yB: e.tensor_add(
                        yfj, yA, yB), waits=w, inc=("V", 1))
                else:
                    zsl = MISC[:, 3072 + st2 * 512: 3072 + st2 * 512 + 512]
                    ph = ACC[:, :]
                    ww = list(w)
                    if i >= 22:
                        ww.append(("T2", t_zrot[st2]))
                    E("vector", lambda e, zsl=zsl, yA=yA, yB=yB: e.tensor_add(
                        zsl, yA, yB), waits=ww, inc=("V", 1))
                    v_z = C("V")
                    if i == 21:
                        yfj = BIG[0:64, 21 * 1024 + st2 * 512: 21 * 1024 + st2 * 512 + 512]
                        E("vector", lambda e, yfj=yfj, zsl=zsl: e.tensor_copy(
                            yfj, zsl[0:64, :]), inc=("V", 1))
                    E("tensor", lambda e, ph=ph, zsl=zsl: e.matmul(
                        ph, perm[:, 128:256], zsl, start=True, stop=True,
                        skip_group_check=True), waits=[("V", v_z), ("V", C("V"))],
                        inc=("T2", 1))
                    t_zr = C("T2")
                    t_zrot[st2] = t_zr
                    j = i - 21
                    yfj = BIG[0:64, j * 1024 + st2 * 512: j * 1024 + st2 * 512 + 512]
                    E("vector", lambda e, yfj=yfj, ph=ph: e.tensor_add(
                        yfj, yfj, ph[0:64, :]), waits=[("T2", t_zr)])
                    if i >= 22:
                        j2 = i - 22
                        yfj2 = BIG[64:128, j2 * 1024 + st2 * 512: j2 * 1024 + st2 * 512 + 512]
                        E("vector", lambda e, yfj2=yfj2, ph=ph: e.tensor_add(
                            yfj2, yfj2, ph[64:128, :]), inc=("V", 1))
                    else:
                        E("vector", lambda e: e.tensor_copy(lrecb[0:1, 0:512],
                                                            lrecb[0:1, 0:512]), inc=("V", 1))
    for (poc, pst2, pysl, pn) in pend_u2:
        u2bank = [ROW[0:2, 0:512], BCAST[0:2, 0:512]][pst2]
        E("tensor", lambda e, poc=poc, pysl=pysl, u2bank=u2bank, pn=pn: e.matmul(
            u2bank, a2t[:, poc * 2:poc * 2 + 2], pysl,
            start=(pn == 0), stop=(pn == len(oc_list) - 1), skip_group_check=True),
            waits=[("V", v_y[(poc, pst2)])], inc=("T2", 1))
        u2_tgt[(poc, pst2)] = C("T2")
    pend_u2 = []
    t_u2_done = C("T2")
    v_yf_done = C("V")
    E("vector", lambda e: e.tensor_scalar_mul(u2b[0:2, 0:512], ROW[0:2, 0:512], LORA_SCALE),
      waits=[("T2", t_u2_done)])
    E("vector", lambda e: e.tensor_scalar_mul(u2b[0:2, 512:1024], BCAST[0:2, 0:512],
                                              LORA_SCALE), inc=("V", 1))
    v_u2 = C("V")

    if STAGE == 4:
        dump_and_finish(P, tn, BIG, [("V", v_yf_done), ("V", v_u2)])
        return

    w2q3a = w2qT[0:2688, :].rearrange("(kc p) o -> p kc o", p=128)
    t_w2free = {}
    v_xb2 = {}
    t_stats2 = {}
    t_statmm2 = {}
    fidx = 0
    for o2 in range(32):
        buf = o2 % 2
        waits = [("V", v_u2)] if o2 < 2 else [("T", t_w2free[o2 - 2])]
        E("gpsimd", lambda e, o2=o2, buf=buf: e.dma_start(
            out=SLAB[:, buf * 2816:buf * 2816 + 2688].rearrange("p (kc o) -> p kc o", o=128),
            in_=w2q3a[:, :, o2 * 128:(o2 + 1) * 128]), waits=waits, inc=("G", 16))
        E("gpsimd", lambda e, o2=o2, buf=buf: e.dma_start(
            out=SLAB[0:64, buf * 2816 + 2688:buf * 2816 + 2816],
            in_=w2qT[2688:2752, o2 * 128:(o2 + 1) * 128]), inc=("G", 16))
        E("gpsimd", lambda e, o2=o2, buf=buf: e.dma_start(
            out=b2t[0:2, buf * 128:(buf + 1) * 128], in_=tn["b2Th"][:, o2 * 128:(o2 + 1) * 128]),
            inc=("G", 16))
        g_w2slab = C("G")
        for st2 in range(2):
            bank = [ACC, BCAST][fidx % 2]
            for k2 in range(21):
                waits = []
                if k2 == 0:
                    waits.append(("G", g_w2slab))
                    if fidx < 2:
                        waits.append(("V", v_yf_done))
                        waits.append(("V", v_u2))
                    if fidx >= 2:
                        waits.append(("V", v_xb2[fidx - 2]))
                E("tensor", lambda e, bank=bank, buf=buf, k2=k2, st2=st2: e.matmul(
                    bank[:, :], SLAB[:, buf * 2816 + k2 * 128:buf * 2816 + (k2 + 1) * 128],
                    BIG[:, k2 * 1024 + st2 * 512: k2 * 1024 + st2 * 512 + 512],
                    start=(k2 == 0), stop=False, skip_group_check=True), waits=waits)
            E("tensor", lambda e, bank=bank, buf=buf, st2=st2: e.matmul(
                bank[:, :], SLAB[0:64, buf * 2816 + 2688:buf * 2816 + 2816],
                BIG[0:64, 21 * 1024 + st2 * 512: 21 * 1024 + st2 * 512 + 512],
                start=False, stop=False, skip_group_check=True))
            E("tensor", lambda e, bank=bank, buf=buf, st2=st2: e.matmul(
                bank[:, :], b2t[0:2, buf * 128:(buf + 1) * 128],
                u2b[0:2, st2 * 512:(st2 + 1) * 512],
                start=False, stop=True, skip_group_check=True), inc=("T", 1))
            t_f = C("T")
            if st2 == 1:
                t_w2free[o2] = t_f
            xcols = QT[:, o2 * 1024 + st2 * 512: o2 * 1024 + st2 * 512 + 512]
            sq = MISC[:, 2560 + (fidx % 2) * 512: 2560 + (fidx % 2) * 512 + 512]
            E("vector", lambda e, xcols=xcols, bank=bank: e.tensor_add(xcols, xcols, bank[:, :]),
              waits=[("T", t_f)])
            waits = [] if fidx < 2 else [("T", t_statmm2[fidx - 2])]
            E("vector", lambda e, xcols=xcols, sq=sq: e.tensor_mul(sq, xcols, xcols),
              waits=waits, inc=("V", 1))
            v_xb2[fidx] = C("V")
            ps = [S0, S1][st2]
            E("tensor", lambda e, ps=ps, xcols=xcols, o2=o2: e.matmul(
                ps[0:1, 0:512], onesc[:, :], xcols,
                start=(o2 == 0), stop=(o2 == 31), skip_group_check=True),
                waits=[("V", v_xb2[fidx])])
            E("tensor", lambda e, ps=ps, o2=o2, sq=sq: e.matmul(
                ps[0:1, 512:1024], onesc[:, :], sq,
                start=(o2 == 0), stop=(o2 == 31), skip_group_check=True), inc=("T", 1))
            t_statmm2[fidx] = C("T")
            if o2 == 31:
                t_stats2[st2] = C("T")
            fidx += 1

    rb_banks2 = {0: ACC[:, :], 1: S0[:, 0:512]}
    mb_banks2 = {0: BCAST[:, :], 1: S0[:, 512:1024]}
    t_bcast2 = {}
    t_bcast2[0] = emit_finalize(0, t_stats2[0], rb_banks2[0], mb_banks2[0], 0)
    t_bcast2[1] = emit_finalize(1, t_stats2[1], rb_banks2[1], mb_banks2[1], 0)

    oidx = 0
    for st2 in range(2):
        for c in range(32):
            xcols = QT[:, c * 1024 + st2 * 512: c * 1024 + st2 * 512 + 512]
            oslot = FSCR[:, 2048 + (oidx % 2) * 512: 2048 + (oidx % 2) * 512 + 512]
            waits = [("T2", t_bcast2[st2])]
            if oidx >= 2:
                waits.append(("O", (oidx - 1) * 16))
            E("vector", lambda e, xcols=xcols, rb=rb_banks2[st2]: e.tensor_mul(
                FSCR[:, 1536:2048], xcols, rb), waits=waits)
            E("vector", lambda e, oslot=oslot, mb=mb_banks2[st2]: e.tensor_sub(
                oslot, FSCR[:, 1536:2048], mb), inc=("V", 1))
            v_o = C("V")
            E("sync", lambda e, c=c, st2=st2, oslot=oslot: e.dma_start(
                out=outT[c * 128:(c + 1) * 128, st2 * 512:(st2 + 1) * 512], in_=oslot),
                waits=[("V", v_o)], inc=("O", 16))
            oidx += 1
    E("sync", None, waits=[("O", C("O"))])


def dump_and_finish(P, tn, src, waits):
    E = P.emit
    outT = tn["outT"]
    out3 = outT.rearrange("(c p) s -> p c s", p=128)
    src3 = src.ap().rearrange("p (c s) -> p c s", s=1024)
    for q in range(4):
        E("gpsimd", lambda e, q=q: e.dma_start(
            out=out3[:, q * 8:(q + 1) * 8, :], in_=src3[:, q * 8:(q + 1) * 8, :]),
            waits=waits if q == 0 else (), inc=("O", 16))
    E("sync", None, waits=[("O", P.cnt("O"))])


def build_nc():
    nc = bass.Bass("TRN2", target_bir_lowering=False)
    tn = {}
    tn["hsT"] = nc.dram_tensor("hsT", [D, SC], F32, kind="ExternalInput").ap()
    tn["wq"] = nc.dram_tensor("wq", [D, D], F32, kind="ExternalInput").ap()
    tn["wo"] = nc.dram_tensor("wo", [D, D], F32, kind="ExternalInput").ap()
    tn["kvc"] = nc.dram_tensor("kvc", [1024, 2, NKV, 16, HD], F32, kind="ExternalInput").ap()
    tn["bt"] = nc.dram_tensor("bt", [512], I32, kind="ExternalInput").ap()
    tn["w1qT"] = nc.dram_tensor("w1qT", [1024, HF], F32, kind="ExternalInput").ap()
    tn["w3qT"] = nc.dram_tensor("w3qT", [1024, HF], F32, kind="ExternalInput").ap()
    tn["w2qT"] = nc.dram_tensor("w2qT", [KF2, D], F32, kind="ExternalInput").ap()
    tn["a1th"] = nc.dram_tensor("a1th", [128, 64], F32, kind="ExternalInput").ap()
    tn["a3th"] = nc.dram_tensor("a3th", [128, 64], F32, kind="ExternalInput").ap()
    tn["a2th"] = nc.dram_tensor("a2th", [128, 172], F32, kind="ExternalInput").ap()
    tn["b1Th"] = nc.dram_tensor("b1Th", [2, HF], F32, kind="ExternalInput").ap()
    tn["b3Th"] = nc.dram_tensor("b3Th", [2, HF], F32, kind="ExternalInput").ap()
    tn["b2Th"] = nc.dram_tensor("b2Th", [2, D], F32, kind="ExternalInput").ap()
    tn["rc_h"] = nc.dram_tensor("rc_h", [128, SC], BF16, kind="ExternalInput").ap()
    tn["rs_h"] = nc.dram_tensor("rs_h", [128, SC], BF16, kind="ExternalInput").ap()
    tn["onesc_h"] = nc.dram_tensor("onesc_h", [128, 1], BF16, kind="ExternalInput").ap()
    tn["onesr_h"] = nc.dram_tensor("onesr_h", [1, 128], BF16, kind="ExternalInput").ap()
    tn["ident_h"] = nc.dram_tensor("ident_h", [128, 128], BF16, kind="ExternalInput").ap()
    tn["perm_h"] = nc.dram_tensor("perm_h", [128, 256], BF16, kind="ExternalInput").ap()
    tn["outT"] = nc.dram_tensor("outT", [D, SC], F32, kind="ExternalOutput").ap()

    with ExitStack() as ctx:
        sbt = lambda name, shape, dt: ctx.enter_context(nc.sbuf_tensor(name, shape, dt))
        tn["QT"] = sbt("QT", [128, 32768], BF16)
        tn["BIG"] = sbt("BIG", [128, 32768], BF16)
        tn["SLAB"] = sbt("SLAB", [128, 8192], BF16)
        tn["MISC"] = sbt("MISC", [128, 4096], BF16)
        tn["FSCR"] = sbt("FSCR", [128, 3072], F32)
        tn["cos"] = sbt("cos", [128, SC], BF16)
        tn["sin"] = sbt("sin", [128, SC], BF16)
        tn["onesc"] = sbt("onesc", [128, 1], BF16)
        tn["onesr"] = sbt("onesr", [1, 128], BF16)
        tn["ident"] = sbt("ident", [128, 128], BF16)
        tn["perm"] = sbt("perm", [128, 256], BF16)
        tn["idx_sb"] = sbt("idx_sb", [128, 4], I32)
        tn["bias_sb"] = sbt("bias_sb", [128, 8], F32)
        tn["maxq_list"] = sbt("maxq_list", [1, 64], F32)
        tn["maxk_list"] = sbt("maxk_list", [1, 16], F32)
        tn["knsum"] = sbt("knsum", [128, 64], F32)
        tn["kncol"] = sbt("kncol", [128, 1], F32)
        tn["kncolb"] = sbt("kncolb", [128, 1], BF16)
        tn["maxq"] = sbt("maxq", [1, 1], F32)
        tn["srow"] = sbt("srow", [1, 2048], F32)
        tn["statf"] = sbt("statf", [1, 2048], F32)
        tn["statb"] = sbt("statb", [1, 2112], BF16)
        tn["lrec"] = sbt("lrec", [1, 512], F32)
        tn["dena"] = sbt("dena", [128, 1024], F32)
        tn["denb"] = sbt("denb", [128, 512], BF16)
        tn["lrecb"] = sbt("lrecb", [1, 512], BF16)
        tn["a1t"] = sbt("a1t", [128, 64], BF16)
        tn["a3t"] = sbt("a3t", [128, 64], BF16)
        tn["a2t"] = sbt("a2t", [128, 172], BF16)
        tn["b1t"] = sbt("b1t", [2, 256], BF16)
        tn["b3t"] = sbt("b3t", [2, 256], BF16)
        tn["b2t"] = sbt("b2t", [2, 256], BF16)
        tn["u1b"] = sbt("u1b", [2, 1024], BF16)
        tn["u3b"] = sbt("u3b", [2, 1024], BF16)
        tn["u2b"] = sbt("u2b", [2, 1024], BF16)
        pst = lambda name, shape, dt: ctx.enter_context(nc.psum_tensor(name, shape, dt))
        tn["S0"] = pst("S0", [128, 1024], F32)
        tn["S1"] = pst("S1", [128, 1024], F32)
        tn["ACC"] = pst("ACC", [128, 512], F32)
        tn["BCAST"] = pst("BCAST", [128, 512], F32)
        tn["ROW"] = pst("ROW", [2, 512], F32)
        tn["PTB"] = pst("PTB", [128, 1024], BF16)

        semnames = ["G", "G2", "S", "O", "T", "T2", "V", "V2", "V3", "SC", "S2"]
        semh = {n: ctx.enter_context(nc.semaphore(n)) for n in semnames}

        P = Plan()
        build_plan(P, tn)

        block = ctx.enter_context(nc.Block())

        def replay(engname):
            def body(eng):
                for (e, fn, waits, incs) in P.ops:
                    if e != engname:
                        continue
                    for (s, v) in waits:
                        eng.wait_ge(semh[s], v)
                    if fn is not None:
                        inst = fn(eng)
                        for (s, amt) in incs:
                            inst = inst.then_inc(semh[s], amt)
                    else:
                        for (s, amt) in incs:
                            eng.sem_inc(semh[s], amt)
            return body

        block.sync(replay("sync"))
        block.gpsimd(replay("gpsimd"))
        block.tensor(replay("tensor"))
        block.vector(replay("vector"))
        block.scalar(replay("scalar"))

    return nc


def _host_inputs(inputs):
    hs = np.asarray(inputs["L0_attn_hidden_states"], dtype=np.float32)
    hs2 = hs.reshape(2 * 4096, D)
    hsT = np.ascontiguousarray(hs2.T)
    wq = np.ascontiguousarray(np.asarray(inputs["L0_attn_Wq"], dtype=np.float32))
    wo = np.ascontiguousarray(np.asarray(inputs["L0_attn_Wo"], dtype=np.float32))
    kvc = np.ascontiguousarray(np.asarray(inputs["L0_attn_kv_cache"], dtype=np.float32))
    bt = np.ascontiguousarray(np.asarray(inputs["L0_attn_block_table"], dtype=np.int32))
    w1qT = np.ascontiguousarray(np.asarray(inputs["L0_ffn_w1q"], dtype=np.float32).T)
    w3qT = np.ascontiguousarray(np.asarray(inputs["L0_ffn_w3q"], dtype=np.float32).T)
    w2qT = np.ascontiguousarray(np.asarray(inputs["L0_ffn_w2q"], dtype=np.float32).T)
    a1 = np.asarray(inputs["L0_ffn_a1"], dtype=np.float32)
    a3 = np.asarray(inputs["L0_ffn_a3"], dtype=np.float32)
    a2 = np.asarray(inputs["L0_ffn_a2"], dtype=np.float32)
    a1th = np.ascontiguousarray(a1.reshape(2, 32, 128).transpose(2, 1, 0).reshape(128, 64))
    a3th = np.ascontiguousarray(a3.reshape(2, 32, 128).transpose(2, 1, 0).reshape(128, 64))
    a2th = np.ascontiguousarray(a2.reshape(2, 86, 128).transpose(2, 1, 0).reshape(128, 172))
    b1Th = np.ascontiguousarray(np.asarray(inputs["L0_ffn_b1"], dtype=np.float32).T)
    b3Th = np.ascontiguousarray(np.asarray(inputs["L0_ffn_b3"], dtype=np.float32).T)
    b2Th = np.ascontiguousarray(np.asarray(inputs["L0_ffn_b2"], dtype=np.float32).T)

    onesc = np.ones((128, 1), dtype=ml_dtypes.bfloat16)
    pm = np.zeros((128, 128), dtype=np.float32)
    for i in range(64):
        pm[64 + i, i] = -1.0
        pm[i, 64 + i] = 1.0
    sh = np.zeros((128, 128), dtype=np.float32)
    for i in range(64):
        sh[64 + i, i] = 1.0
        sh[i, 64 + i] = 1.0
    perm = np.concatenate([pm, sh], axis=1).astype(ml_dtypes.bfloat16)
    onesr = np.ones((1, 128), dtype=ml_dtypes.bfloat16)
    ident = np.eye(128, dtype=ml_dtypes.bfloat16)

    invf = 10000.0 ** (-(np.arange(64, dtype=np.float64)) / 64.0)
    invf2 = np.concatenate([invf, invf])

    in_maps = []
    for core in range(NCORE):
        pos = (core % 4) * 1024 + np.arange(1024, dtype=np.float64)
        ang = invf2[:, None] * pos[None, :]
        rc = np.cos(ang).astype(ml_dtypes.bfloat16)
        rs = np.sin(ang).astype(ml_dtypes.bfloat16)
        m = {
            "hsT": np.ascontiguousarray(hsT[:, core * SC:(core + 1) * SC]),
            "wq": wq, "wo": wo, "kvc": kvc, "bt": bt,
            "w1qT": w1qT, "w3qT": w3qT, "w2qT": w2qT,
            "a1th": a1th, "a3th": a3th, "a2th": a2th,
            "b1Th": b1Th, "b3Th": b3Th, "b2Th": b2Th,
            "rc_h": rc, "rs_h": rs,
            "onesc_h": onesc, "onesr_h": onesr, "ident_h": ident, "perm_h": perm,
        }
        in_maps.append(m)
    return in_maps


def kernel(**inputs):
    in_maps = _host_inputs(inputs)
    nc = build_nc()
    res = run_bass_kernel_spmd(nc, in_maps, core_ids=list(range(NCORE)))
    outT = np.concatenate([r["outT"] for r in res.results], axis=1)
    return np.ascontiguousarray(outT.T).reshape(2, 4096, D).astype(np.float32)
